# version 01
# baseline (speedup 1.0000x reference)
import numpy as np
import ml_dtypes
from contextlib import ExitStack

import concourse.bass as bass
import concourse.mybir as mybir
from concourse.bass_utils import run_bass_kernel_spmd

F32 = mybir.dt.float32
BF16 = mybir.dt.bfloat16
AF = mybir.ActivationFunctionType
ALU = mybir.AluOpType
AX = mybir.AxisListType.X
BF16NP = ml_dtypes.bfloat16

B, T, D = 4, 4096, 1024
DI = 2048
DS = 16
TLOC = 2048
NCORES = 8
TC = 512


def f32c(x):
    return np.ascontiguousarray(x, dtype=np.float32)


def cols128(v):
    return f32c(np.asarray(v, np.float32).reshape(-1, 128).T)


def ceil_div(a, b):
    return (a + b - 1) // b


class Cnt(dict):
    def bump(self, k, n=1):
        self[k] = self.get(k, 0) + n
        return self[k]

    def g(self, k):
        return self.get(k, 0)


class PhaseList:
    def __init__(self):
        self.phases = []

    def add(self, p):
        self.phases.append(p)
        return p

    def emit_all(self, block):
        for engname in ('gpsimd', 'sync', 'tensor', 'scalar', 'vector'):
            def mk(en):
                def run(eng):
                    for p in self.phases:
                        p.emit(en, eng)
                return run
            getattr(block, engname)(mk(engname))


class RawPhase:
    def __init__(self, emitters):
        self.emitters = emitters

    def emit(self, engname, eng):
        f = self.emitters.get(engname)
        if f is not None:
            f(eng)


class MMPhase:
    def __init__(self, S, C, wT_dram, K, N, cols, act_fn, wpanel, psums,
                 evac_fn, evac_engines=('scalar',), c_off=0, part_base=0,
                 extra_tensor_waits=()):
        self.S, self.C = S, C
        self.wT, self.K, self.N = wT_dram, K, N
        self.act_fn, self.wpanel, self.psums = act_fn, wpanel, psums
        self.evac_fn = evac_fn
        self.evac_engines = evac_engines
        self.pb = part_base
        self.kt = ceil_div(K, 128)
        self.supers = [(n0, min(512, N - n0)) for n0 in range(0, N, 512)]
        self.cgs = [(c0 + c_off, min(512, cols - c0))
                    for c0 in range(0, cols, 512)]
        self.groups = []
        for si, (n0, nw) in enumerate(self.supers):
            for nt in range(ceil_div(nw, 128)):
                nr = min(128, nw - nt * 128)
                for (c0, cw) in self.cgs:
                    self.groups.append((si, n0 + nt * 128, nr, c0, cw))
        self.extra_tw = list(extra_tensor_waits)
        self.base_wl = [C.g('wl0'), C.g('wl1')]
        self.base_mm = C.g('mm')
        self.base_ev = C.g('ev')
        ns = len(self.supers)
        C.bump('wl0', 16 * self.kt * ((ns + 1) // 2))
        C.bump('wl1', 16 * self.kt * (ns // 2))
        C.bump('mm', len(self.groups))
        C.bump('ev', len(self.groups))

    def ktile_rows(self, k):
        return min(128, self.K - k * 128)

    def mm_thru_super(self, si):
        return self.base_mm + sum(1 for g in self.groups if g[0] <= si)

    def emit(self, engname, eng):
        if engname == 'gpsimd':
            self._emit_gpsimd(eng)
        elif engname == 'tensor':
            self._emit_tensor(eng)
        if engname in self.evac_engines:
            self._emit_evac(engname, eng)

    def _emit_gpsimd(self, g):
        S = self.S
        g.wait_ge(S['mm'], self.base_mm)
        for si, (n0, nw) in enumerate(self.supers):
            if si >= 2:
                g.wait_ge(S['mm'], self.mm_thru_super(si - 2))
            for k in range(self.kt):
                kr = self.ktile_rows(k)
                g.dma_start(
                    out=self.wpanel[si % 2][self.pb:self.pb + kr,
                                            k * 512:k * 512 + nw],
                    in_=self.wT[k * 128:k * 128 + kr, n0:n0 + nw],
                ).then_inc(S['wl0' if si % 2 == 0 else 'wl1'], 16)

    def _emit_tensor(self, t):
        S = self.S
        t.wait_ge(S['ev'], self.base_ev)
        for sem, val in self.extra_tw:
            t.wait_ge(S[sem], val)
        nps = len(self.psums)
        cur_si = -1
        for gi, (si, n_abs, nr, c0, cw) in enumerate(self.groups):
            if si != cur_si:
                par = si % 2
                t.wait_ge(S['wl0' if par == 0 else 'wl1'],
                          self.base_wl[par] + 16 * self.kt * (si // 2 + 1))
                cur_si = si
            ps = self.psums[gi % nps]
            if gi >= nps:
                t.wait_ge(S['ev'], self.base_ev + gi - nps + 1)
            panel = self.wpanel[si % 2]
            n_loc = n_abs - self.supers[si][0]
            for k in range(self.kt):
                kr = self.ktile_rows(k)
                ins = t.matmul(
                    ps[0:nr, 0:cw],
                    panel[self.pb:self.pb + kr,
                          k * 512 + n_loc:k * 512 + n_loc + nr],
                    self.act_fn(k)[:, c0:c0 + cw],
                    start=(k == 0), stop=(k == self.kt - 1),
                )
                if k == self.kt - 1:
                    ins.then_inc(S['mm'], 1)

    def _emit_evac(self, engname, eng):
        S = self.S
        nps = len(self.psums)
        for gi, (si, n_abs, nr, c0, cw) in enumerate(self.groups):
            eng.wait_ge(S['mm'], self.base_mm + gi + 1)
            self.evac_fn(engname, eng, self.psums[gi % nps],
                         n_abs, nr, c0, cw, gi)


class LNPhase:
    def __init__(self, S, C, s_bf_fn, s_fn, out_fn, cw, aux, pre=None):
        self.S, self.C = S, C
        self.s_bf_fn, self.s_fn, self.out_fn = s_bf_fn, s_fn, out_fn
        self.cw, self.aux = cw, aux
        self.pre = pre or {}
        self.base_sq = C.g('sq'); C.bump('sq', 8)
        self.base_lnm = C.g('lnm'); C.bump('lnm', 1)
        self.base_lnv = C.g('lnv'); C.bump('lnv', 1)
        self.base_lns = C.g('lns'); C.bump('lns', 1)
        self.base_lnb = C.g('lnb'); C.bump('lnb', 2)
        self.base_lne = C.g('lne'); C.bump('lne', 8)
        self.base_squ = C.g('squ'); C.bump('squ', 8)

    def emit(self, engname, eng):
        S, A, cw = self.S, self.aux, self.cw
        for sem, val in self.pre.get(engname, ()):
            eng.wait_ge(S[sem], val)
        if engname == 'scalar':
            for k in range(8):
                if k >= 2:
                    eng.wait_ge(S['squ'], self.base_squ + k - 1)
                eng.activation(A['sq_bf'][k % 2][:, 0:cw],
                               self.s_bf_fn(k)[:, 0:cw],
                               AF.Square).then_inc(S['sq'], 1)
            eng.wait_ge(S['lnv'], self.base_lnv + 1)
            eng.activation(A['row_t'][0:1, 0:cw], A['row_v'][0:1, 0:cw],
                           AF.Ln, bias=A['eps'][0:1, 0:1])
            eng.activation(A['row_r'][0:1, 0:cw], A['row_t'][0:1, 0:cw],
                           AF.Exp, bias=0.0, scale=A['neghalf'][0:1, 0:1]
                           ).then_inc(S['lns'], 1)
        elif engname == 'tensor':
            eng.wait_ge(S['lnv'], self.base_lnv)
            eng.wait_ge(S['lne'], self.base_lne)
            for k in range(8):
                ins = eng.matmul(A['ps_m'][0:1, 0:cw], A['ones128_bf'][:, 0:1],
                                 self.s_bf_fn(k)[:, 0:cw],
                                 start=(k == 0), stop=(k == 7))
                if k == 7:
                    ins.then_inc(S['lnm'], 1)
            for k in range(8):
                eng.wait_ge(S['sq'], self.base_sq + k + 1)
                ins = eng.matmul(A['ps_q'][0:1, 0:cw], A['ones128_bf'][:, 0:1],
                                 A['sq_bf'][k % 2][:, 0:cw],
                                 start=(k == 0), stop=(k == 7))
                ins.then_inc(S['squ'], 1)
            eng.wait_ge(S['lns'], self.base_lns + 1)
            eng.matmul(A['ps_mb'][:, 0:cw], A['ones1_f32'][0:1, :],
                       A['row_m'][0:1, 0:cw], start=True, stop=True
                       ).then_inc(S['lnb'], 1)
            eng.matmul(A['ps_rb'][:, 0:cw], A['ones1_f32'][0:1, :],
                       A['row_r'][0:1, 0:cw], start=True, stop=True
                       ).then_inc(S['lnb'], 1)
        elif engname == 'vector':
            eng.wait_ge(S['lnm'], self.base_lnm + 1)
            eng.wait_ge(S['squ'], self.base_squ + 8)
            eng.tensor_scalar_mul(A['row_m'][0:1, 0:cw], A['ps_m'][0:1, 0:cw],
                                  1.0 / 1024.0)
            eng.tensor_scalar_mul(A['row_v'][0:1, 0:cw], A['ps_q'][0:1, 0:cw],
                                  1.0 / 1024.0)
            eng.tensor_mul(A['row_t'][0:1, 0:cw], A['row_m'][0:1, 0:cw],
                           A['row_m'][0:1, 0:cw])
            eng.tensor_sub(A['row_v'][0:1, 0:cw], A['row_v'][0:1, 0:cw],
                           A['row_t'][0:1, 0:cw]).then_inc(S['lnv'], 1)
            eng.wait_ge(S['lnb'], self.base_lnb + 2)
            for k in range(8):
                eng.tensor_sub(A['nrm'][k % 2][:, 0:cw], self.s_fn(k)[:, 0:cw],
                               A['ps_mb'][:, 0:cw])
                eng.tensor_mul(self.out_fn(k)[:, 0:cw],
                               A['nrm'][k % 2][:, 0:cw],
                               A['ps_rb'][:, 0:cw]).then_inc(S['lne'], 1)


def build_A1():
    nc = bass.Bass("TRN2", target_bir_lowering=False)
    xw = nc.dram_tensor("xw", [D, TLOC + 4], F32, kind="ExternalInput").ap()
    w_in = nc.dram_tensor("w_in", [D, 2 * DI], F32, kind="ExternalInput").ap()
    wx_main = nc.dram_tensor("wx_main", [DI, DI], F32, kind="ExternalInput").ap()
    wx_small = nc.dram_tensor("wx_small", [DI, 32], F32, kind="ExternalInput").ap()
    w_dt = nc.dram_tensor("w_dt", [DS, DI], F32, kind="ExternalInput").ap()
    convw_d = nc.dram_tensor("convw", [128, 64], F32, kind="ExternalInput").ap()
    convb_d = nc.dram_tensor("convb", [128, 16], F32, kind="ExternalInput").ap()
    bdt_d = nc.dram_tensor("bdt", [128, 16], F32, kind="ExternalInput").ap()
    selb_d = nc.dram_tensor("selb", [32, 1], F32, kind="ExternalInput").ap()
    G_o = nc.dram_tensor("G", [DI, TLOC], F32, kind="ExternalOutput").ap()
    P_o = nc.dram_tensor("P", [DI, TLOC], F32, kind="ExternalOutput").ap()
    M1_o = nc.dram_tensor("M1", [DI, TLOC], F32, kind="ExternalOutput").ap()

    S = {}
    C = Cnt()
    PL = PhaseList()
    with ExitStack() as cs:
        e = cs.enter_context
        for sname in ('wl0', 'wl1', 'mm', 'ev', 'xl', 'cl', 'cv', 'xs', 'dv',
                      'sb', 'sbr', 'sbc', 'st', 'sm0', 'sm1'):
            S[sname] = e(nc.semaphore(sname))
        x_bf = [e(nc.sbuf_tensor(f"xbf{k}", [128, 516], BF16)) for k in range(8)]
        xin = [e(nc.sbuf_tensor(f"xin{j}", [128, 516], F32)) for j in range(16)]
        sres = [e(nc.sbuf_tensor(f"sres{j}", [128, 516], F32)) for j in range(16)]
        xc32 = [e(nc.sbuf_tensor(f"xc32{j}", [128, 512], F32)) for j in range(16)]
        xcbf = [e(nc.sbuf_tensor(f"xcbf{j}", [128, 512], BF16)) for j in range(16)]
        wpanel = [e(nc.sbuf_tensor(f"wp{i}", [128, 16 * 512], BF16)) for i in range(2)]
        small32 = e(nc.sbuf_tensor("small32", [32, 512], F32))
        small32_bf = e(nc.sbuf_tensor("small32bf", [32, 512], BF16))
        sb_row = e(nc.sbuf_tensor("sbrow", [1, 512], F32))
        ctap = [e(nc.sbuf_tensor(f"ctap{i}", [128, 512], F32)) for i in range(2)]
        m1buf = [e(nc.sbuf_tensor(f"m1b{i}", [128, 512], F32)) for i in range(2)]
        tA = [e(nc.sbuf_tensor(f"tA{i}", [128, 512], F32)) for i in range(2)]
        tB = [e(nc.sbuf_tensor(f"tB{i}", [128, 512], F32)) for i in range(2)]
        dbuf = [e(nc.sbuf_tensor(f"dbuf{i}", [128, 512], F32)) for i in range(2)]
        ctbuf = [e(nc.sbuf_tensor(f"ctb{i}", [128, 512], F32)) for i in range(2)]
        gbuf = [e(nc.sbuf_tensor(f"gbuf{i}", [128, 512], F32)) for i in range(2)]
        pbuf = [e(nc.sbuf_tensor(f"pbuf{i}", [128, 512], F32)) for i in range(2)]
        gcarry = e(nc.sbuf_tensor("gcarry", [128, 16], F32))
        pcarry = e(nc.sbuf_tensor("pcarry", [128, 16], F32))
        zeros512 = e(nc.sbuf_tensor("zeros512", [128, 512], F32))
        convw_t = e(nc.sbuf_tensor("convwt", [128, 64], F32))
        convb_t = e(nc.sbuf_tensor("convbt", [128, 16], F32))
        bdt_t = e(nc.sbuf_tensor("bdtt", [128, 16], F32))
        selb_bf = e(nc.sbuf_tensor("selbbf", [32, 1], BF16))
        ones1 = e(nc.sbuf_tensor("ones1", [1, 128], F32))
        psA = [e(nc.psum_tensor(f"psA{i}", [128, 512], F32)) for i in range(2)]
        ps_sb = e(nc.psum_tensor("pssb", [1, 512], F32))
        ps_sbc = e(nc.psum_tensor("pssbc", [128, 512], F32))

        def consts_g(g):
            g.dma_start(out=convw_t[:, :], in_=convw_d).then_inc(S['cl'], 16)
            g.dma_start(out=convb_t[:, :], in_=convb_d).then_inc(S['cl'], 16)
            g.dma_start(out=bdt_t[:, :], in_=bdt_d).then_inc(S['cl'], 16)
            g.dma_start(out=selb_bf[:, :], in_=selb_d).then_inc(S['cl'], 16)

        def consts_v(v):
            v.memset(ones1[0:1, :], 1.0)
            v.memset(zeros512[:, :], 0.0)
            v.memset(gcarry[:, :], 0.0)
            v.memset(pcarry[:, :], 1.0)
            v.wait_ge(S['cl'], 64)

        def consts_s(s):
            s.wait_ge(S['cl'], 64)

        def consts_t(t):
            t.wait_ge(S['cl'], 64)
        C.bump('cl', 64)
        PL.add(RawPhase({'gpsimd': consts_g, 'vector': consts_v,
                         'scalar': consts_s, 'tensor': consts_t}))

        for ci in range(4):
            c0d = ci * 512

            prev_mm = C.g('mm')

            def loadx(g, ci=ci, prev_mm=prev_mm):
                if ci > 0:
                    g.wait_ge(S['mm'], prev_mm)
                for k in range(8):
                    g.dma_start(out=x_bf[k][:, :],
                                in_=xw[k * 128:(k + 1) * 128,
                                       ci * 512:ci * 512 + 516]
                                ).then_inc(S['xl'], 16)
            C.bump('xl', 128)
            xl_after = C.g('xl')
            PL.add(RawPhase({'gpsimd': loadx}))

            st_mm1 = {'conv_j': 0, 'silu_j': 0}
            base_cv = C.g('cv')
            base_xs = C.g('xs')

            def mk_mm1_evac(base_cv=base_cv, base_xs=base_xs, st=st_mm1,
                            ph_holder=[]):
                def evac(engname, eng, ps, n_abs, nr, c0, cw, gi):
                    ph = ph_holder[0]
                    if n_abs < DI:
                        j = n_abs // 128
                        if engname == 'scalar':
                            eng.activation(xin[j][:, c0:c0 + cw], ps[0:nr, 0:cw],
                                           AF.Copy).then_inc(S['ev'], 1)
                            if c0 == 512:
                                if j >= 1:
                                    eng.wait_ge(S['cv'], base_cv + j)
                                    eng.activation(xc32[j - 1][:, :],
                                                   ctap[(j - 1) % 2][:, :],
                                                   AF.Silu,
                                                   bias=convb_t[:, j - 1:j])
                                    eng.activation(xcbf[j - 1][:, :],
                                                   xc32[j - 1][:, :],
                                                   AF.Copy).then_inc(S['xs'], 1)
                        elif engname == 'vector':
                            if c0 == 512:
                                j = n_abs // 128
                                eng.wait_ge(S['ev'], ph.base_ev + gi + 1)
                                if j >= 2:
                                    eng.wait_ge(S['xs'], base_xs + j - 1)
                                eng.tensor_scalar_mul(
                                    ctap[j % 2][:, :], xin[j][:, 1:513],
                                    convw_t[:, 4 * j:4 * j + 1])
                                for k in range(1, 4):
                                    ins = eng.scalar_tensor_tensor(
                                        ctap[j % 2][:, :],
                                        xin[j][:, 1 + k:513 + k],
                                        convw_t[:, 4 * j + k:4 * j + k + 1],
                                        ctap[j % 2][:, :], ALU.mult, ALU.add)
                                ins.then_inc(S['cv'], 1)
                    else:
                        jr = (n_abs - DI) // 128
                        if engname == 'scalar':
                            eng.activation(sres[jr][:, c0:c0 + cw],
                                           ps[0:nr, 0:cw],
                                           AF.Silu).then_inc(S['ev'], 1)
                            if gi == 32:
                                eng.wait_ge(S['cv'], base_cv + 16)
                                eng.activation(xc32[15][:, :], ctap[15 % 2][:, :],
                                               AF.Silu, bias=convb_t[:, 15:16])
                                eng.activation(xcbf[15][:, :], xc32[15][:, :],
                                               AF.Copy).then_inc(S['xs'], 1)
                return evac, ph_holder

            evac1, holder1 = mk_mm1_evac()
            mm1 = MMPhase(S, C, w_in, 1024, 4096, 516, lambda k: x_bf[k][:, :],
                          wpanel, psA, evac1, evac_engines=('scalar', 'vector'),
                          extra_tensor_waits=[('xl', xl_after)])
            holder1.append(mm1)
            C.bump('cv', 16)
            C.bump('xs', 16)
            PL.add(mm1)

            base_st2 = C.g('st')

            base_sm = [C.g('sm0'), C.g('sm1')]

            def mk_mm2_evac(base_sm=base_sm, c0d=c0d, ph_holder=[]):
                def evac(engname, eng, ps, n_abs, nr, c0, cw, gi):
                    j = n_abs // 128
                    if engname == 'vector':
                        par = gi % 2
                        eng.wait_ge(S['sm0' if par == 0 else 'sm1'],
                                    base_sm[par] + 16 * (gi // 2))
                        eng.tensor_mul(m1buf[gi % 2][:, 0:cw], ps[0:nr, 0:cw],
                                       sres[j][:, 4:4 + cw]).then_inc(S['ev'], 1)
                    elif engname == 'sync':
                        eng.wait_ge(S['ev'], ph_holder[0].base_ev + gi + 1)
                        eng.dma_start(out=M1_o[j * 128:(j + 1) * 128,
                                               c0d:c0d + cw],
                                      in_=m1buf[gi % 2][:, 0:cw]
                                      ).then_inc(S['sm0' if gi % 2 == 0 else 'sm1'], 16)
                return evac, ph_holder

            evac2, holder2 = mk_mm2_evac()
            mm2 = MMPhase(S, C, wx_main, DI, DI, 512,
                          lambda k: xcbf[k][:, :], wpanel, psA, evac2,
                          evac_engines=('vector', 'sync'),
                          extra_tensor_waits=[('xs', C.g('xs'))])
            holder2.append(mm2)
            C.bump('sm0', 16 * 8)
            C.bump('sm1', 16 * 8)
            PL.add(mm2)

            def evac3(engname, eng, ps, n_abs, nr, c0, cw, gi):
                eng.activation(small32[0:32, 0:cw], ps[0:32, 0:cw], AF.Copy)
                eng.activation(small32_bf[0:32, 0:cw], ps[0:32, 0:cw],
                               AF.Copy).then_inc(S['ev'], 1)

            mm3 = MMPhase(S, C, wx_small, DI, 32, 512,
                          lambda k: xcbf[k][:, :], wpanel, psA, evac3,
                          evac_engines=('scalar',))
            PL.add(mm3)

            base_dv = C.g('dv')
            sb_n = C.g('sb')

            def sb_t(t, after_ev=C.g('ev'), base_dv=base_dv, sb_n=sb_n):
                t.wait_ge(S['ev'], after_ev)
                t.wait_ge(S['dv'], base_dv)
                t.matmul(ps_sb[0:1, 0:512], selb_bf[0:32, 0:1],
                         small32_bf[0:32, 0:512], start=True, stop=True
                         ).then_inc(S['sb'], 1)
                t.wait_ge(S['sbr'], sb_n + 1)
                t.matmul(ps_sbc[:, 0:512], ones1[0:1, 0:128],
                         sb_row[0:1, 0:512], start=True, stop=True
                         ).then_inc(S['sbc'], 1)

            def sb_s(s, sb_n=sb_n):
                s.wait_ge(S['sb'], sb_n + 1)
                s.activation(sb_row[0:1, 0:512], ps_sb[0:1, 0:512],
                             AF.Copy).then_inc(S['sbr'], 1)
            C.bump('sb', 1)
            C.bump('sbr', 1)
            C.bump('sbc', 1)
            PL.add(RawPhase({'tensor': sb_t, 'scalar': sb_s}))

            base_dv4 = C.g('dv')
            base_st4 = C.g('st')
            sbc_n = C.g('sbc')

            def mk_mm4_evac(base_dv4=base_dv4, base_st4=base_st4, c0d=c0d,
                            sbc_n=sbc_n, ph_holder=[]):
                def evac(engname, eng, ps, n_abs, nr, c0, cw, gi):
                    j = gi
                    if engname == 'scalar':
                        eng.wait_ge(S['dv'], base_dv4 + max(0, j - 1))
                        eng.activation(tA[j % 2][:, 0:cw], ps[0:nr, 0:cw],
                                       AF.Abs, bias=bdt_t[:, j:j + 1])
                        eng.activation(tB[j % 2][:, 0:cw], tA[j % 2][:, 0:cw],
                                       AF.Exp, bias=0.0, scale=-1.0)
                        eng.activation(tA[j % 2][:, 0:cw], tB[j % 2][:, 0:cw],
                                       AF.Ln, bias=1.0)
                        eng.activation(tB[j % 2][:, 0:cw], ps[0:nr, 0:cw],
                                       AF.Relu, bias=bdt_t[:, j:j + 1]
                                       ).then_inc(S['ev'], 1)
                    elif engname == 'vector':
                        eng.wait_ge(S['ev'], ph_holder[0].base_ev + gi + 1)
                        eng.wait_ge(S['sbc'], sbc_n)
                        eng.wait_ge(S['st'], base_st4 + 32 * max(0, j - 1))
                        eng.tensor_add(dbuf[j % 2][:, 0:cw], tA[j % 2][:, 0:cw],
                                       tB[j % 2][:, 0:cw])
                        eng.tensor_mul(ctbuf[j % 2][:, 0:cw], xc32[j][:, 0:cw],
                                       ps_sbc[:, 0:cw])
                        eng.tensor_tensor_scan(
                            gbuf[j % 2][:, 0:cw], dbuf[j % 2][:, 0:cw],
                            ctbuf[j % 2][:, 0:cw], gcarry[:, j:j + 1],
                            ALU.mult, ALU.add)
                        eng.tensor_tensor_scan(
                            pbuf[j % 2][:, 0:cw], dbuf[j % 2][:, 0:cw],
                            zeros512[:, 0:cw], pcarry[:, j:j + 1],
                            ALU.mult, ALU.add)
                        eng.tensor_copy(gcarry[:, j:j + 1],
                                        gbuf[j % 2][:, cw - 1:cw])
                        eng.tensor_copy(pcarry[:, j:j + 1],
                                        pbuf[j % 2][:, cw - 1:cw]
                                        ).then_inc(S['dv'], 1)
                    elif engname == 'sync':
                        eng.wait_ge(S['dv'], base_dv4 + j + 1)
                        eng.dma_start(out=G_o[j * 128:(j + 1) * 128,
                                              c0d:c0d + cw],
                                      in_=gbuf[j % 2][:, 0:cw]
                                      ).then_inc(S['st'], 16)
                        eng.dma_start(out=P_o[j * 128:(j + 1) * 128,
                                              c0d:c0d + cw],
                                      in_=pbuf[j % 2][:, 0:cw]
                                      ).then_inc(S['st'], 16)
                return evac, ph_holder

            evac4, holder4 = mk_mm4_evac()
            mm4 = MMPhase(S, C, w_dt, DS, DI, 512,
                          lambda k: small32_bf[0:16, :], wpanel, psA, evac4,
                          evac_engines=('scalar', 'vector', 'sync'))
            holder4.append(mm4)
            C.bump('dv', 16)
            C.bump('st', 32 * 16)
            PL.add(mm4)

        st_total = C.g('st')
        sm0_t, sm1_t = C.g('sm0'), C.g('sm1')

        def fin(s):
            s.wait_ge(S['st'], st_total)
            s.wait_ge(S['sm0'], sm0_t)
            s.wait_ge(S['sm1'], sm1_t)
        PL.add(RawPhase({'sync': fin}))

        block = e(nc.Block())
        PL.emit_all(block)
    return nc


def host_prep_A1(inp):
    x = np.asarray(inp["L0_attn_x"], np.float32)
    w_in = f32c(inp["L0_attn_W_in"])
    wx = np.asarray(inp["L0_attn_W_x"], np.float32)
    wx_main = f32c(wx[:, 32:2080])
    wx_small = f32c(np.concatenate([wx[:, 0:16], wx[:, 16:32]], axis=1))
    w_dt = f32c(inp["L0_attn_W_dt"])
    convw = f32c(np.asarray(inp["L0_attn_conv_w"], np.float32)
                 .reshape(DI, 4).reshape(16, 128, 4)
                 .transpose(1, 0, 2).reshape(128, 64))
    convb = cols128(inp["L0_attn_conv_b"])
    bdt = cols128(inp["L0_attn_b_dt"])
    selb = np.zeros((32, 1), np.float32)
    selb[16:32, 0] = 1.0
    maps = []
    for c in range(NCORES):
        b, h = c // 2, c % 2
        t0 = h * TLOC
        xw = np.zeros((D, TLOC + 4), np.float32)
        lo = max(0, t0 - 4)
        xw[:, 4 - (t0 - lo):] = x[b, lo:t0 + TLOC, :].T
        maps.append(dict(xw=f32c(xw), w_in=w_in, wx_main=wx_main,
                         wx_small=wx_small, w_dt=w_dt, convw=convw,
                         convb=convb, bdt=bdt, selb=selb))
    return maps


def build_A2(debug=False):
    nc = bass.Bass("TRN2", target_bir_lowering=False)
    G_i = nc.dram_tensor("G", [DI, TLOC], F32, kind="ExternalInput").ap()
    P_i = nc.dram_tensor("P", [DI, TLOC], F32, kind="ExternalInput").ap()
    M1_i = nc.dram_tensor("M1", [DI, TLOC], F32, kind="ExternalInput").ap()
    ccol_d = nc.dram_tensor("ccol", [128, 16], F32, kind="ExternalInput").ap()
    xT_d = nc.dram_tensor("xT", [D, TLOC], F32, kind="ExternalInput").ap()
    wout_d = nc.dram_tensor("wout", [DI, D], F32, kind="ExternalInput").ap()
    ln1_o = nc.dram_tensor("ln1", [D, TLOC], F32, kind="ExternalOutput").ap()
    if debug:
        y2_dbg = nc.dram_tensor("y2dbg", [DI, TLOC], BF16, kind="ExternalOutput").ap()
        s_dbg = nc.dram_tensor("sdbg", [D, TLOC], F32, kind="ExternalOutput").ap()
        var_dbg = nc.dram_tensor("vardbg", [4, TLOC], F32, kind="ExternalOutput").ap()

    S = {}
    C = Cnt()
    PL = PhaseList()
    with ExitStack() as cs:
        e = cs.enter_context
        for sname in ('wl0', 'wl1', 'mm', 'ev', 'xl0', 'xl1', 'xf', 'cl',
                      'yv', 'sv', 'st',
                      'sq', 'squ', 'lnm', 'lnv', 'lns', 'lnb', 'lne'):
            S[sname] = e(nc.semaphore(sname))
        grot = [e(nc.sbuf_tensor(f"gr{i}", [128, 512], F32)) for i in range(2)]
        prot = [e(nc.sbuf_tensor(f"pr{i}", [128, 512], F32)) for i in range(2)]
        mrot = [e(nc.sbuf_tensor(f"mr{i}", [128, 512], F32)) for i in range(2)]
        y2t = [e(nc.sbuf_tensor(f"y2t{i}", [128, 512], F32)) for i in range(2)]
        y2bf = [e(nc.sbuf_tensor(f"y2bf{j}", [128, 512], BF16)) for j in range(16)]
        x32 = [e(nc.sbuf_tensor(f"x32{k}", [128, 512], F32)) for k in range(8)]
        s32 = [e(nc.sbuf_tensor(f"s32{k}", [128, 512], F32)) for k in range(8)]
        sbf = [e(nc.sbuf_tensor(f"sbf{k}", [128, 512], BF16)) for k in range(8)]
        lnout = [e(nc.sbuf_tensor(f"lno{k}", [128, 512], F32)) for k in range(8)]
        wpanel = [e(nc.sbuf_tensor(f"wp{i}", [128, 16 * 512], BF16)) for i in range(2)]
        ccol_t = e(nc.sbuf_tensor("ccolt", [128, 16], F32))
        aux = dict(
            ones128_bf=e(nc.sbuf_tensor("o128", [128, 1], BF16)),
            ones1_f32=e(nc.sbuf_tensor("o1f", [1, 128], F32)),
            sq_bf=[e(nc.sbuf_tensor(f"sqbf{i}", [128, 512], BF16)) for i in range(2)],
            row_m=e(nc.sbuf_tensor("row_m", [1, 512], F32)),
            row_v=e(nc.sbuf_tensor("row_v", [1, 512], F32)),
            row_t=e(nc.sbuf_tensor("row_t", [1, 512], F32)),
            row_r=e(nc.sbuf_tensor("row_r", [1, 512], F32)),
            eps=e(nc.sbuf_tensor("cst_eps", [1, 1], F32)),
            neghalf=e(nc.sbuf_tensor("cst_nh", [1, 1], F32)),
            nrm=[e(nc.sbuf_tensor(f"nrm{i}", [128, 512], F32)) for i in range(2)],
            ps_m=e(nc.psum_tensor("psm", [1, 512], F32)),
            ps_q=e(nc.psum_tensor("psq", [1, 512], F32)),
            ps_mb=e(nc.psum_tensor("psmb", [128, 512], F32)),
            ps_rb=e(nc.psum_tensor("psrb", [128, 512], F32)),
        )
        psA = [e(nc.psum_tensor(f"psA{i}", [128, 512], F32)) for i in range(2)]

        def consts_g(g):
            g.dma_start(out=ccol_t[:, :], in_=ccol_d).then_inc(S['cl'], 16)

        def consts_v(v):
            v.memset(aux['ones1_f32'][0:1, :], 1.0)
            v.memset(aux['ones128_bf'][:, 0:1], 1.0)
            v.memset(aux['eps'][0:1, 0:1], 1e-5)
            v.memset(aux['neghalf'][0:1, 0:1], -0.5)
            v.wait_ge(S['cl'], 16)
        C.bump('cl', 16)
        PL.add(RawPhase({'gpsimd': consts_g, 'vector': consts_v}))

        for ci in range(4):
            c0d = ci * 512

            base_xl = [C.g('xl0'), C.g('xl1')]
            base_yv = C.g('yv')
            prev_mm = C.g('mm')
            prev_sv = C.g('sv')

            def loads(s, base_yv=base_yv, c0d=c0d, ci=ci, prev_mm=prev_mm,
                      prev_sv=prev_sv):
                if ci > 0:
                    s.wait_ge(S['sv'], prev_sv)
                for j in range(16):
                    if j >= 2:
                        s.wait_ge(S['yv'], base_yv + j - 1)
                    sem = S['xl0' if j % 2 == 0 else 'xl1']
                    s.dma_start(out=grot[j % 2][:, :],
                                in_=G_i[j * 128:(j + 1) * 128, c0d:c0d + 512]
                                ).then_inc(sem, 16)
                    s.dma_start(out=prot[j % 2][:, :],
                                in_=P_i[j * 128:(j + 1) * 128, c0d:c0d + 512]
                                ).then_inc(sem, 16)
                    s.dma_start(out=mrot[j % 2][:, :],
                                in_=M1_i[j * 128:(j + 1) * 128, c0d:c0d + 512]
                                ).then_inc(sem, 16)
                for k in range(8):
                    s.dma_start(out=x32[k][:, :],
                                in_=xT_d[k * 128:(k + 1) * 128, c0d:c0d + 512]
                                ).then_inc(S['xf'], 16)
            C.bump('xl0', 48 * 8)
            C.bump('xl1', 48 * 8)
            C.bump('xf', 16 * 8)
            PL.add(RawPhase({'sync': loads}))

            def y2ph(v, base_xl=base_xl, base_yv=base_yv, prev_mm=prev_mm,
                     ci=ci):
                if ci > 0:
                    v.wait_ge(S['mm'], prev_mm)
                for j in range(16):
                    par = j % 2
                    v.wait_ge(S['xl0' if par == 0 else 'xl1'],
                              base_xl[par] + 48 * (j // 2 + 1))
                    v.scalar_tensor_tensor(
                        y2t[j % 2][:, :], prot[j % 2][:, :],
                        ccol_t[:, j:j + 1], grot[j % 2][:, :],
                        ALU.mult, ALU.add)
                    v.tensor_mul(y2bf[j][:, :], y2t[j % 2][:, :],
                                 mrot[j % 2][:, :]).then_inc(S['yv'], 1)
            C.bump('yv', 16)
            PL.add(RawPhase({'vector': y2ph}))

            xl_all = C.g('xf')

            def mk_evac5(ph_holder=[], xl_all=xl_all):
                def evac(engname, eng, ps, n_abs, nr, c0, cw, gi):
                    k = n_abs // 128
                    if engname == 'vector':
                        if gi == 0:
                            eng.wait_ge(S['xf'], xl_all)
                        eng.tensor_add(s32[k][:, 0:cw], ps[0:nr, 0:cw],
                                       x32[k][:, 0:cw]).then_inc(S['ev'], 1)
                        eng.tensor_copy(sbf[k][:, 0:cw], s32[k][:, 0:cw]
                                        ).then_inc(S['sv'], 1)
                return evac, ph_holder

            evac5, holder5 = mk_evac5()
            mm5 = MMPhase(S, C, wout_d, DI, D, 512,
                          lambda k: y2bf[k][:, :], wpanel, psA, evac5,
                          evac_engines=('vector',),
                          extra_tensor_waits=[('yv', C.g('yv'))])
            holder5.append(mm5)
            C.bump('sv', 8)
            PL.add(mm5)

            base_st = C.g('st')
            ln = LNPhase(S, C, lambda k: sbf[k], lambda k: s32[k],
                         lambda k: lnout[k], 512, aux,
                         pre={'tensor': [('sv', C.g('sv'))],
                              'scalar': [('sv', C.g('sv'))],
                              'vector': [('st', base_st)]})
            PL.add(ln)

            def store(s, base=ln.base_lne, c0d=c0d):
                for k in range(8):
                    s.wait_ge(S['lne'], base + k + 1)
                    s.dma_start(out=ln1_o[k * 128:(k + 1) * 128, c0d:c0d + 512],
                                in_=lnout[k][:, :]).then_inc(S['st'], 16)
            C.bump('st', 16 * 8)
            PL.add(RawPhase({'sync': store}))

            if debug:
                def dstore(s, c0d=c0d, lne_end=C.g('lne')):
                    s.wait_ge(S['lne'], lne_end)
                    for j in range(16):
                        s.dma_start(out=y2_dbg[j * 128:(j + 1) * 128,
                                               c0d:c0d + 512],
                                    in_=y2bf[j][:, :]).then_inc(S['st'], 16)
                    for k in range(8):
                        s.dma_start(out=s_dbg[k * 128:(k + 1) * 128,
                                              c0d:c0d + 512],
                                    in_=s32[k][:, :]).then_inc(S['st'], 16)
                    s.dma_start(out=var_dbg[0:1, c0d:c0d + 512],
                                in_=aux['row_m'][0:1, :]).then_inc(S['st'], 16)
                    s.dma_start(out=var_dbg[1:2, c0d:c0d + 512],
                                in_=aux['row_v'][0:1, :]).then_inc(S['st'], 16)
                    s.dma_start(out=var_dbg[2:3, c0d:c0d + 512],
                                in_=aux['row_r'][0:1, :]).then_inc(S['st'], 16)
                C.bump('st', 16 * 27)
                PL.add(RawPhase({'sync': dstore}))

        st_total = C.g('st')
        PL.add(RawPhase({'sync': lambda s: s.wait_ge(S['st'], st_total)}))
        block = e(nc.Block())
        PL.emit_all(block)
    return nc


def host_prep_A2(inp, a1_results):
    x = np.asarray(inp["L0_attn_x"], np.float32)
    wout = f32c(inp["L0_attn_W_out"])
    maps = []
    for c in range(NCORES):
        b, h = c // 2, c % 2
        t0 = h * TLOC
        if h == 0:
            ccol = np.zeros((128, 16), np.float32)
        else:
            ccol = cols128(a1_results[c - 1]["G"][:, -1])
        maps.append(dict(G=a1_results[c]["G"], P=a1_results[c]["P"],
                         M1=a1_results[c]["M1"], ccol=ccol,
                         xT=f32c(x[b, t0:t0 + TLOC, :].T), wout=wout))
    return maps


def build_B():
    nc = bass.Bass("TRN2", target_bir_lowering=False)
    ln1_i = nc.dram_tensor("ln1", [D, TLOC], F32, kind="ExternalInput").ap()
    pcol_d = nc.dram_tensor("pcol", [128, 8], F32, kind="ExternalInput").ap()
    mur_d = nc.dram_tensor("mur", [128, 8], F32, kind="ExternalInput").ap()
    muk_d = nc.dram_tensor("muk", [128, 8], F32, kind="ExternalInput").ap()
    wrT_d = nc.dram_tensor("wrT", [D, D], F32, kind="ExternalInput").ap()
    wkT_d = nc.dram_tensor("wkT", [D, D], F32, kind="ExternalInput").ap()
    wvT_d = nc.dram_tensor("wvT", [D, D], F32, kind="ExternalInput").ap()
    ln2_o = nc.dram_tensor("ln2", [D, TLOC], F32, kind="ExternalOutput").ap()

    S = {}
    C = Cnt()
    PL = PhaseList()
    with ExitStack() as cs:
        e = cs.enter_context
        for sname in ('wl0', 'wl1', 'mm', 'ev', 'xl', 'cl', 'lp', 'sv', 'st',
                      'sq', 'squ', 'lnm', 'lnv', 'lns', 'lnb', 'lne'):
            S[sname] = e(nc.semaphore(sname))
        x32 = [e(nc.sbuf_tensor(f"x32{k}", [128, 512], F32)) for k in range(8)]
        xp = [e(nc.sbuf_tensor(f"xp{k}", [128, 512], F32)) for k in range(8)]
        dt_t = [e(nc.sbuf_tensor(f"dt{i}", [128, 512], F32)) for i in range(2)]
        xr32 = [e(nc.sbuf_tensor(f"xr32{i}", [128, 512], F32)) for i in range(2)]
        xrbf = [e(nc.sbuf_tensor(f"xrbf{k}", [128, 512], BF16)) for k in range(8)]
        xkbf = [e(nc.sbuf_tensor(f"xkbf{k}", [128, 512], BF16)) for k in range(8)]
        r32 = [e(nc.sbuf_tensor(f"r32{k}", [128, 512], F32)) for k in range(8)]
        kabf = [e(nc.sbuf_tensor(f"kabf{k}", [128, 512], BF16)) for k in range(8)]
        t32 = [e(nc.sbuf_tensor(f"t32{i}", [128, 512], F32)) for i in range(2)]
        t32v = [e(nc.sbuf_tensor(f"t32v{i}", [128, 512], F32)) for i in range(2)]
        s32 = [e(nc.sbuf_tensor(f"s32{k}", [128, 512], F32)) for k in range(8)]
        sbf = [e(nc.sbuf_tensor(f"sbf{k}", [128, 512], BF16)) for k in range(8)]
        lnout = [e(nc.sbuf_tensor(f"lno{k}", [128, 512], F32)) for k in range(8)]
        wpanel = [e(nc.sbuf_tensor(f"wp{i}", [128, 8 * 512], BF16)) for i in range(2)]
        pcol_t = e(nc.sbuf_tensor("pcolt", [128, 8], F32))
        mur_t = e(nc.sbuf_tensor("murt", [128, 8], F32))
        muk_t = e(nc.sbuf_tensor("mukt", [128, 8], F32))
        aux = dict(
            ones128_bf=e(nc.sbuf_tensor("o128", [128, 1], BF16)),
            ones1_f32=e(nc.sbuf_tensor("o1f", [1, 128], F32)),
            sq_bf=[e(nc.sbuf_tensor(f"sqbf{i}", [128, 512], BF16)) for i in range(2)],
            row_m=e(nc.sbuf_tensor("row_m", [1, 512], F32)),
            row_v=e(nc.sbuf_tensor("row_v", [1, 512], F32)),
            row_t=e(nc.sbuf_tensor("row_t", [1, 512], F32)),
            row_r=e(nc.sbuf_tensor("row_r", [1, 512], F32)),
            eps=e(nc.sbuf_tensor("cst_eps", [1, 1], F32)),
            neghalf=e(nc.sbuf_tensor("cst_nh", [1, 1], F32)),
            nrm=[e(nc.sbuf_tensor(f"nrm{i}", [128, 512], F32)) for i in range(2)],
            ps_m=e(nc.psum_tensor("psm", [1, 512], F32)),
            ps_q=e(nc.psum_tensor("psq", [1, 512], F32)),
            ps_mb=e(nc.psum_tensor("psmb", [128, 512], F32)),
            ps_rb=e(nc.psum_tensor("psrb", [128, 512], F32)),
        )
        psA = [e(nc.psum_tensor(f"psA{i}", [128, 512], F32)) for i in range(2)]

        def consts_g(g):
            g.dma_start(out=pcol_t[:, :], in_=pcol_d).then_inc(S['cl'], 16)
            g.dma_start(out=mur_t[:, :], in_=mur_d).then_inc(S['cl'], 16)
            g.dma_start(out=muk_t[:, :], in_=muk_d).then_inc(S['cl'], 16)

        def consts_v(v):
            v.memset(aux['ones1_f32'][0:1, :], 1.0)
            v.memset(aux['ones128_bf'][:, 0:1], 1.0)
            v.memset(aux['eps'][0:1, 0:1], 1e-5)
            v.memset(aux['neghalf'][0:1, 0:1], -0.5)
            v.wait_ge(S['cl'], 48)
        C.bump('cl', 48)
        PL.add(RawPhase({'gpsimd': consts_g, 'vector': consts_v}))

        for ci in range(4):
            c0d = ci * 512

            base_xl = C.g('xl')
            prev_mm = C.g('mm')

            prev_sv = C.g('sv')
            prev_lne = C.g('lne')

            def loads(s, c0d=c0d, ci=ci, prev_mm=prev_mm, prev_sv=prev_sv,
                      prev_lne=prev_lne):
                if ci > 0:
                    s.wait_ge(S['mm'], prev_mm)
                    s.wait_ge(S['sv'], prev_sv)
                    s.wait_ge(S['lne'], prev_lne)
                for k in range(8):
                    s.dma_start(out=x32[k][:, :],
                                in_=ln1_i[k * 128:(k + 1) * 128, c0d:c0d + 512]
                                ).then_inc(S['xl'], 16)
            C.bump('xl', 16 * 8)
            PL.add(RawPhase({'sync': loads}))

            def lerp(v, base_xl=base_xl, ci=ci, xl_all=C.g('xl')):
                v.wait_ge(S['xl'], xl_all)
                for k in range(8):
                    v.tensor_copy(xp[k][:, 1:512], x32[k][:, 0:511])
                    v.tensor_copy(xp[k][:, 0:1], pcol_t[:, k:k + 1])
                    v.tensor_copy(pcol_t[:, k:k + 1], x32[k][:, 511:512])
                    v.tensor_sub(dt_t[k % 2][:, :], x32[k][:, :], xp[k][:, :])
                    v.scalar_tensor_tensor(xr32[k % 2][:, :], dt_t[k % 2][:, :],
                                           mur_t[:, k:k + 1], xp[k][:, :],
                                           ALU.mult, ALU.add)
                    v.tensor_copy(xrbf[k][:, :], xr32[k % 2][:, :])
                    v.scalar_tensor_tensor(xr32[k % 2][:, :], dt_t[k % 2][:, :],
                                           muk_t[:, k:k + 1], xp[k][:, :],
                                           ALU.mult, ALU.add)
                    v.tensor_copy(xkbf[k][:, :], xr32[k % 2][:, :]
                                  ).then_inc(S['lp'], 1)
            C.bump('lp', 8)
            PL.add(RawPhase({'vector': lerp}))

            def evac_r(engname, eng, ps, n_abs, nr, c0, cw, gi):
                eng.activation(r32[n_abs // 128][:, 0:cw], ps[0:nr, 0:cw],
                               AF.Sigmoid).then_inc(S['ev'], 1)

            mmr = MMPhase(S, C, wrT_d, D, D, 512, lambda k: xrbf[k][:, :],
                          wpanel, psA, evac_r, evac_engines=('scalar',),
                          extra_tensor_waits=[('lp', C.g('lp'))])
            PL.add(mmr)

            def evac_k(engname, eng, ps, n_abs, nr, c0, cw, gi):
                k = n_abs // 128
                eng.activation(t32[gi % 2][:, 0:cw], ps[0:nr, 0:cw], AF.Relu)
                eng.activation(kabf[k][:, 0:cw], t32[gi % 2][:, 0:cw],
                               AF.Square).then_inc(S['ev'], 1)

            mmk = MMPhase(S, C, wkT_d, D, D, 512, lambda k: xkbf[k][:, :],
                          wpanel, psA, evac_k, evac_engines=('scalar',))
            PL.add(mmk)

            def evac_v(engname, eng, ps, n_abs, nr, c0, cw, gi):
                k = n_abs // 128
                eng.tensor_mul(t32v[gi % 2][:, 0:cw], ps[0:nr, 0:cw],
                               r32[k][:, 0:cw]).then_inc(S['ev'], 1)
                eng.tensor_add(s32[k][:, 0:cw], t32v[gi % 2][:, 0:cw],
                               x32[k][:, 0:cw])
                eng.tensor_copy(sbf[k][:, 0:cw], s32[k][:, 0:cw]
                                ).then_inc(S['sv'], 1)

            mmv = MMPhase(S, C, wvT_d, D, D, 512, lambda k: kabf[k][:, :],
                          wpanel, psA, evac_v, evac_engines=('vector',))
            C.bump('sv', 8)
            PL.add(mmv)

            base_st = C.g('st')
            ln = LNPhase(S, C, lambda k: sbf[k], lambda k: s32[k],
                         lambda k: lnout[k], 512, aux,
                         pre={'tensor': [('sv', C.g('sv'))],
                              'scalar': [('sv', C.g('sv'))],
                              'vector': [('st', base_st)]})
            PL.add(ln)

            def store(s, base=ln.base_lne, c0d=c0d):
                for k in range(8):
                    s.wait_ge(S['lne'], base + k + 1)
                    s.dma_start(out=ln2_o[k * 128:(k + 1) * 128, c0d:c0d + 512],
                                in_=lnout[k][:, :]).then_inc(S['st'], 16)
            C.bump('st', 16 * 8)
            PL.add(RawPhase({'sync': store}))

        st_total = C.g('st')
        PL.add(RawPhase({'sync': lambda s: s.wait_ge(S['st'], st_total)}))
        block = e(nc.Block())
        PL.emit_all(block)
    return nc


def host_prep_B(inp, a2_results):
    wrT = f32c(np.asarray(inp["L0_ffn_W_r"], np.float32).T)
    wkT = f32c(np.asarray(inp["L0_ffn_W_k"], np.float32).T)
    wvT = f32c(np.asarray(inp["L0_ffn_W_v"], np.float32).T)
    mur = cols128(inp["L0_ffn_mu_r"])
    muk = cols128(inp["L0_ffn_mu_k"])
    maps = []
    for c in range(NCORES):
        h = c % 2
        if h == 0:
            pcol = np.zeros((128, 8), np.float32)
        else:
            pcol = cols128(a2_results[c - 1]["ln1"][:, -1])
        maps.append(dict(ln1=a2_results[c]["ln1"], pcol=pcol, mur=mur,
                         muk=muk, wrT=wrT, wkT=wkT, wvT=wvT))
    return maps


def build_C():
    nc = bass.Bass("TRN2", target_bir_lowering=False)
    xT_d = nc.dram_tensor("xT", [D, T + 4], F32, kind="ExternalInput").ap()
    winT_d = nc.dram_tensor("winT", [D, 2 * DI], F32, kind="ExternalInput").ap()
    wxT_d = nc.dram_tensor("wxT", [DI, 128], F32, kind="ExternalInput").ap()
    wdtT_d = nc.dram_tensor("wdtT", [64, DI], F32, kind="ExternalInput").ap()
    convw_d = nc.dram_tensor("convw", [128, 64], F32, kind="ExternalInput").ap()
    convb_d = nc.dram_tensor("convb", [128, 16], F32, kind="ExternalInput").ap()
    bdt_d = nc.dram_tensor("bdt", [128, 16], F32, kind="ExternalInput").ap()
    u_o = nc.dram_tensor("u", [DI, T], F32, kind="ExternalOutput").ap()
    sz_o = nc.dram_tensor("sz", [DI, T], F32, kind="ExternalOutput").ap()
    delta_o = nc.dram_tensor("delta", [DI, T], F32, kind="ExternalOutput").ap()
    bc_o = nc.dram_tensor("bc", [32, T], F32, kind="ExternalOutput").ap()

    S = {}
    C = Cnt()
    PL = PhaseList()
    with ExitStack() as cs:
        e = cs.enter_context
        for sname in ('wl0', 'wl1', 'mm', 'ev', 'xl', 'cl', 'cv', 'xs',
                      'dv', 'st', 'sd0', 'sd1', 'su0', 'su1', 'sz0', 'sz1'):
            S[sname] = e(nc.semaphore(sname))
        x_bf = [e(nc.sbuf_tensor(f"xbf{k}", [128, 516], BF16)) for k in range(8)]
        upre = [e(nc.sbuf_tensor(f"up{j}", [128, 516], F32)) for j in range(16)]
        zbuf = [e(nc.sbuf_tensor(f"zb{i}", [128, 516], F32)) for i in range(2)]
        u32 = [e(nc.sbuf_tensor(f"u32{i}", [128, 512], F32)) for i in range(2)]
        u_bf = [e(nc.sbuf_tensor(f"ubf{j}", [128, 512], BF16)) for j in range(16)]
        ctap = [e(nc.sbuf_tensor(f"ctap{i}", [128, 512], F32)) for i in range(2)]
        small96 = e(nc.sbuf_tensor("s96", [128, 512], F32))
        small96_bf = e(nc.sbuf_tensor("s96bf", [128, 512], BF16))
        tA = [e(nc.sbuf_tensor(f"tA{i}", [128, 512], F32)) for i in range(2)]
        tB = [e(nc.sbuf_tensor(f"tB{i}", [128, 512], F32)) for i in range(2)]
        dbuf = [e(nc.sbuf_tensor(f"dbuf{i}", [128, 512], F32)) for i in range(2)]
        wpanel = [e(nc.sbuf_tensor(f"wp{i}", [128, 16 * 512], BF16)) for i in range(2)]
        convw_t = e(nc.sbuf_tensor("convwt", [128, 64], F32))
        convb_t = e(nc.sbuf_tensor("convbt", [128, 16], F32))
        bdt_t = e(nc.sbuf_tensor("bdtt", [128, 16], F32))
        psA = [e(nc.psum_tensor(f"psA{i}", [128, 512], F32)) for i in range(2)]

        def consts_g(g):
            g.dma_start(out=convw_t[:, :], in_=convw_d).then_inc(S['cl'], 16)
            g.dma_start(out=convb_t[:, :], in_=convb_d).then_inc(S['cl'], 16)
            g.dma_start(out=bdt_t[:, :], in_=bdt_d).then_inc(S['cl'], 16)

        def consts_w(eng):
            eng.wait_ge(S['cl'], 48)
        C.bump('cl', 48)
        PL.add(RawPhase({'gpsimd': consts_g, 'vector': consts_w,
                         'scalar': consts_w, 'tensor': consts_w}))

        for ci in range(8):
            c0d = ci * 512

            prev_mm = C.g('mm')

            def loadx(g, ci=ci, prev_mm=prev_mm):
                if ci > 0:
                    g.wait_ge(S['mm'], prev_mm)
                for k in range(8):
                    g.dma_start(out=x_bf[k][:, :],
                                in_=xT_d[k * 128:(k + 1) * 128,
                                         ci * 512:ci * 512 + 516]
                                ).then_inc(S['xl'], 16)
            C.bump('xl', 128)
            xl_after = C.g('xl')
            PL.add(RawPhase({'gpsimd': loadx}))

            base_cv = C.g('cv')
            base_xs = C.g('xs')
            base_su = [C.g('su0'), C.g('su1')]
            base_sz = [C.g('sz0'), C.g('sz1')]

            def mk_evwin(base_cv=base_cv, base_xs=base_xs, base_su=base_su,
                         base_sz=base_sz, c0d=c0d, ph_holder=[]):
                def evac(engname, eng, ps, n_abs, nr, c0, cw, gi):
                    ph = ph_holder[0]
                    if n_abs < DI:
                        j = n_abs // 128
                        if engname == 'scalar':
                            eng.activation(upre[j][:, c0:c0 + cw],
                                           ps[0:nr, 0:cw],
                                           AF.Copy).then_inc(S['ev'], 1)
                            if c0 == 512:
                                if j >= 1:
                                    jj = j - 1
                                    eng.wait_ge(S['cv'], base_cv + j)
                                    par = jj % 2
                                    eng.wait_ge(S['su0' if par == 0 else 'su1'],
                                                base_su[par] + 16 * (jj // 2))
                                    eng.activation(u32[jj % 2][:, :],
                                                   ctap[jj % 2][:, :], AF.Silu,
                                                   bias=convb_t[:, jj:jj + 1])
                                    eng.activation(u_bf[jj][:, :],
                                                   u32[jj % 2][:, :],
                                                   AF.Copy).then_inc(S['xs'], 1)
                        elif engname == 'vector':
                            if c0 == 512:
                                j = n_abs // 128
                                eng.wait_ge(S['ev'], ph.base_ev + gi + 1)
                                if j >= 2:
                                    eng.wait_ge(S['xs'], base_xs + j - 1)
                                eng.tensor_scalar_mul(
                                    ctap[j % 2][:, :], upre[j][:, 1:513],
                                    convw_t[:, 4 * j:4 * j + 1])
                                for k in range(1, 4):
                                    ins = eng.scalar_tensor_tensor(
                                        ctap[j % 2][:, :],
                                        upre[j][:, 1 + k:513 + k],
                                        convw_t[:, 4 * j + k:4 * j + k + 1],
                                        ctap[j % 2][:, :], ALU.mult, ALU.add)
                                ins.then_inc(S['cv'], 1)
                        elif engname == 'sync':
                            if c0 == 512:
                                j = n_abs // 128
                                if j >= 1:
                                    jj = j - 1
                                    eng.wait_ge(S['xs'], base_xs + jj + 1)
                                    eng.dma_start(
                                        out=u_o[jj * 128:(jj + 1) * 128,
                                                c0d:c0d + 512],
                                        in_=u32[jj % 2][:, :]
                                        ).then_inc(S['su0' if jj % 2 == 0
                                                   else 'su1'], 16)
                    else:
                        jr = (n_abs - DI) // 128
                        if engname == 'scalar':
                            if gi == 32:
                                eng.wait_ge(S['cv'], base_cv + 16)
                                eng.wait_ge(S['su1'], base_su[1] + 16 * 7)
                                eng.activation(u32[15 % 2][:, :],
                                               ctap[15 % 2][:, :], AF.Silu,
                                               bias=convb_t[:, 15:16])
                                eng.activation(u_bf[15][:, :],
                                               u32[15 % 2][:, :],
                                               AF.Copy).then_inc(S['xs'], 1)
                            if gi % 2 == 0:
                                par = jr % 2
                                eng.wait_ge(S['sz0' if par == 0 else 'sz1'],
                                            base_sz[par] + 16 * (jr // 2))
                            eng.activation(zbuf[jr % 2][:, c0:c0 + cw],
                                           ps[0:nr, 0:cw],
                                           AF.Silu).then_inc(S['ev'], 1)
                        elif engname == 'sync':
                            if c0 == 512:
                                eng.wait_ge(S['ev'], ph.base_ev + gi + 1)
                                eng.dma_start(
                                    out=sz_o[jr * 128:(jr + 1) * 128,
                                             c0d:c0d + 512],
                                    in_=zbuf[jr % 2][:, 4:516]
                                    ).then_inc(S['sz0' if jr % 2 == 0
                                               else 'sz1'], 16)
                            if gi == 63:
                                eng.wait_ge(S['xs'], base_xs + 16)
                                eng.dma_start(
                                    out=u_o[15 * 128:16 * 128, c0d:c0d + 512],
                                    in_=u32[15 % 2][:, :]
                                    ).then_inc(S['su1'], 16)
                return evac, ph_holder

            evwin, hwin = mk_evwin()
            mmwin = MMPhase(S, C, winT_d, D, 2 * DI, 516,
                            lambda k: x_bf[k][:, :], wpanel, psA, evwin,
                            evac_engines=('scalar', 'vector', 'sync'),
                            extra_tensor_waits=[('xl', xl_after)])
            hwin.append(mmwin)
            C.bump('cv', 16)
            C.bump('xs', 16)
            C.bump('su0', 16 * 8)
            C.bump('su1', 16 * 8)
            C.bump('sz0', 16 * 8)
            C.bump('sz1', 16 * 8)
            PL.add(mmwin)

            def evwx(engname, eng, ps, n_abs, nr, c0, cw, gi):
                eng.activation(small96[0:128, 0:cw], ps[0:128, 0:cw], AF.Copy)
                eng.activation(small96_bf[0:128, 0:cw], ps[0:128, 0:cw],
                               AF.Copy).then_inc(S['ev'], 1)

            mmwx = MMPhase(S, C, wxT_d, DI, 128, 512,
                           lambda k: u_bf[k][:, :], wpanel, psA, evwx,
                           evac_engines=('scalar',),
                           extra_tensor_waits=[('xs', C.g('xs'))])
            PL.add(mmwx)

            ev_afterwx = C.g('ev')

            def store_bc(s, ev_afterwx=ev_afterwx, c0d=c0d):
                s.wait_ge(S['ev'], ev_afterwx)
                s.dma_start(out=bc_o[0:32, c0d:c0d + 512],
                            in_=small96[0:32, :]).then_inc(S['st'], 16)
            C.bump('st', 16)
            PL.add(RawPhase({'sync': store_bc}))

            base_dv4 = C.g('dv')
            base_sd = [C.g('sd0'), C.g('sd1')]

            def mk_evdt(base_dv4=base_dv4, base_sd=base_sd, c0d=c0d,
                        ph_holder=[]):
                def evac(engname, eng, ps, n_abs, nr, c0, cw, gi):
                    j = gi
                    if engname == 'scalar':
                        eng.wait_ge(S['dv'], base_dv4 + max(0, j - 1))
                        eng.activation(tA[j % 2][:, 0:cw], ps[0:nr, 0:cw],
                                       AF.Abs, bias=bdt_t[:, j:j + 1])
                        eng.activation(tB[j % 2][:, 0:cw], tA[j % 2][:, 0:cw],
                                       AF.Exp, bias=0.0, scale=-1.0)
                        eng.activation(tA[j % 2][:, 0:cw], tB[j % 2][:, 0:cw],
                                       AF.Ln, bias=1.0)
                        eng.activation(tB[j % 2][:, 0:cw], ps[0:nr, 0:cw],
                                       AF.Relu, bias=bdt_t[:, j:j + 1]
                                       ).then_inc(S['ev'], 1)
                    elif engname == 'vector':
                        eng.wait_ge(S['ev'], ph_holder[0].base_ev + gi + 1)
                        par = j % 2
                        eng.wait_ge(S['sd0' if par == 0 else 'sd1'],
                                    base_sd[par] + 16 * (j // 2))
                        eng.tensor_add(dbuf[j % 2][:, 0:cw], tA[j % 2][:, 0:cw],
                                       tB[j % 2][:, 0:cw]).then_inc(S['dv'], 1)
                    elif engname == 'sync':
                        eng.wait_ge(S['dv'], base_dv4 + j + 1)
                        eng.dma_start(out=delta_o[j * 128:(j + 1) * 128,
                                                  c0d:c0d + 512],
                                      in_=dbuf[j % 2][:, 0:cw]
                                      ).then_inc(S['sd0' if j % 2 == 0
                                                 else 'sd1'], 16)
                return evac, ph_holder

            evdt, hdt = mk_evdt()
            mmdt = MMPhase(S, C, wdtT_d, 64, DI, 512,
                           lambda k: small96_bf[64:128, :], wpanel, psA, evdt,
                           evac_engines=('scalar', 'vector', 'sync'),
                           part_base=64)
            hdt.append(mmdt)
            C.bump('dv', 16)
            C.bump('sd0', 16 * 8)
            C.bump('sd1', 16 * 8)
            PL.add(mmdt)

        def fin(s, tot={k: C.g(k) for k in ('st', 'sd0', 'sd1', 'su0', 'su1',
                                            'sz0', 'sz1')}):
            for k, v in tot.items():
                s.wait_ge(S[k], v)
        PL.add(RawPhase({'sync': fin}))
        block = e(nc.Block())
        PL.emit_all(block)
    return nc


def host_prep_C(inp, ln2_by_core):
    maps = []
    for c in range(NCORES):
        b, dirb = c // 2, c % 2
        sfx = 'b' if dirb else 'f'
        full = np.concatenate([ln2_by_core[2 * b], ln2_by_core[2 * b + 1]],
                              axis=1)
        if dirb:
            full = full[:, ::-1]
        xT = np.zeros((D, T + 4), np.float32)
        xT[:, 4:] = full
        wx = np.asarray(inp[f"L1_attn_wx_{sfx}"], np.float32)
        wxT = np.zeros((DI, 128), np.float32)
        wxT[:, 0:32] = wx[64:96].T
        wxT[:, 64:128] = wx[0:64].T
        wxT = f32c(wxT)
        maps.append(dict(
            xT=f32c(xT),
            winT=f32c(np.asarray(inp[f"L1_attn_win_{sfx}"], np.float32).T),
            wxT=wxT,
            wdtT=f32c(np.asarray(inp[f"L1_attn_wdt_{sfx}"], np.float32).T),
            convw=f32c(np.asarray(inp[f"L1_attn_wconv_{sfx}"], np.float32)
                       .reshape(DI, 4).reshape(16, 128, 4)
                       .transpose(1, 0, 2).reshape(128, 64)),
            convb=cols128(inp[f"L1_attn_bconv_{sfx}"]),
            bdt=cols128(inp[f"L1_attn_bdt_{sfx}"]),
        ))
    return maps


def build_D():
    nc = bass.Bass("TRN2", target_bir_lowering=False)
    u_i = nc.dram_tensor("u", [DI, T], F32, kind="ExternalInput").ap()
    sz_i = nc.dram_tensor("sz", [DI, T], F32, kind="ExternalInput").ap()
    delta_i = nc.dram_tensor("delta", [DI, T], F32, kind="ExternalInput").ap()
    bc_i = nc.dram_tensor("bc", [32, T], F32, kind="ExternalInput").ap()
    alogc_d = nc.dram_tensor("alogc", [128, 256], F32, kind="ExternalInput").ap()
    dpc_d = nc.dram_tensor("dpc", [128, 16], F32, kind="ExternalInput").ap()
    woutT_d = nc.dram_tensor("woutT", [DI, D], F32, kind="ExternalInput").ap()
    oh32_d = nc.dram_tensor("oh32", [32, 32 * 128], F32, kind="ExternalInput").ap()
    i128_d = nc.dram_tensor("i128", [128, 128], F32, kind="ExternalInput").ap()
    od_o = nc.dram_tensor("od", [D, T], F32, kind="ExternalOutput").ap()

    S = {}
    C = Cnt()
    PL = PhaseList()
    with ExitStack() as cs:
        e = cs.enter_context
        for sname in ('wl0', 'wl1', 'mm', 'ev', 'cl', 'bl', 'bm', 'bce',
                      'ld0', 'ld1', 'da', 'hv', 'pv', 'ym', 'sc',
                      'so0', 'so1'):
            S[sname] = e(nc.semaphore(sname))
        Bm_w = e(nc.sbuf_tensor("Bmw", [128, 16 * 512], F32))
        C_w = e(nc.sbuf_tensor("Cw", [128, 16 * 512], F32))
        bc_bf = e(nc.sbuf_tensor("bcbf", [32, 512], BF16))
        da_b = [e(nc.sbuf_tensor(f"dab{i}", [128, 512], F32)) for i in range(2)]
        db_b = [e(nc.sbuf_tensor(f"dbb{i}", [128, 512], F32)) for i in range(2)]
        hb_b = [e(nc.sbuf_tensor(f"hbb{i}", [128, 512], F32)) for i in range(2)]
        ch_b = [e(nc.sbuf_tensor(f"chb{i}", [128, 512], BF16)) for i in range(2)]
        dl = [e(nc.sbuf_tensor(f"dl{i}", [128, 512], F32)) for i in range(2)]
        ul = [e(nc.sbuf_tensor(f"ul{i}", [128, 512], F32)) for i in range(2)]
        zl = [e(nc.sbuf_tensor(f"zl{i}", [128, 512], F32)) for i in range(2)]
        duj = [e(nc.sbuf_tensor(f"duj{i}", [128, 512], F32)) for i in range(2)]
        tp = [e(nc.sbuf_tensor(f"tp{i}", [128, 512], F32)) for i in range(2)]
        ybf = [e(nc.sbuf_tensor(f"ybf{j}", [128, 512], BF16)) for j in range(16)]
        ob = [e(nc.sbuf_tensor(f"ob{i}", [128, 512], F32)) for i in range(2)]
        hcarry = e(nc.sbuf_tensor("hcarry", [128, 256], F32))
        nA = e(nc.sbuf_tensor("nA", [128, 256], F32))
        atmp = e(nc.sbuf_tensor("atmp", [128, 256], F32))
        dpc_t = e(nc.sbuf_tensor("dpct", [128, 16], F32))
        oh_bf = e(nc.sbuf_tensor("ohbf", [32, 32 * 128], BF16))
        i_bf = e(nc.sbuf_tensor("ibf", [128, 128], BF16))
        wpanel = [e(nc.sbuf_tensor(f"wp{i}", [128, 16 * 512], BF16)) for i in range(2)]
        ps_bc = [e(nc.psum_tensor(f"psbc{i}", [128, 512], F32)) for i in range(2)]
        ps_y = [e(nc.psum_tensor(f"psy{i}", [128, 512], F32)) for i in range(2)]
        psA = [e(nc.psum_tensor(f"psA{i}", [128, 512], F32)) for i in range(2)]

        def consts_g(g):
            g.dma_start(out=atmp[:, :], in_=alogc_d).then_inc(S['cl'], 16)
            g.dma_start(out=dpc_t[:, :], in_=dpc_d).then_inc(S['cl'], 16)
            g.dma_start(out=oh_bf[:, :], in_=oh32_d).then_inc(S['cl'], 16)
            g.dma_start(out=i_bf[:, :], in_=i128_d).then_inc(S['cl'], 16)

        def consts_s(s):
            s.wait_ge(S['cl'], 64)
            s.activation(nA[:, :], atmp[:, :], AF.Exp).then_inc(S['cl'], 1)

        def consts_v(v):
            v.memset(hcarry[:, :], 0.0)
            v.wait_ge(S['cl'], 65)
            v.tensor_scalar_mul(nA[:, :], nA[:, :], -1.0).then_inc(S['cl'], 1)

        def consts_t(t):
            t.wait_ge(S['cl'], 66)
        C.bump('cl', 66)
        PL.add(RawPhase({'gpsimd': consts_g, 'scalar': consts_s,
                         'vector': consts_v, 'tensor': consts_t}))

        for ci in range(8):
            c0d = ci * 512
            jj0 = 16 * ci
            gg0 = 256 * ci

            def bcload(g, c0d=c0d, bm_prev=C.g('bm')):
                g.wait_ge(S['bm'], bm_prev)
                g.dma_start(out=bc_bf[:, :], in_=bc_i[0:32, c0d:c0d + 512]
                            ).then_inc(S['bl'], 16)
            C.bump('bl', 16)
            bl_now = C.g('bl')
            PL.add(RawPhase({'gpsimd': bcload}))

            base_bm = C.g('bm')
            base_bce = C.g('bce')

            def bct(t, base_bm=base_bm, base_bce=base_bce, bl_now=bl_now):
                t.wait_ge(S['bl'], bl_now)
                t.wait_ge(S['bce'], base_bce)
                for n in range(32):
                    if n >= 2:
                        t.wait_ge(S['bce'], base_bce + n - 1)
                    t.matmul(ps_bc[n % 2][:, :],
                             oh_bf[0:32, n * 128:(n + 1) * 128],
                             bc_bf[0:32, :], start=True, stop=True
                             ).then_inc(S['bm'], 1)

            def bcs(s, base_bm=base_bm, ci=ci, hv_prev=C.g('hv'),
                    pv_prev=C.g('pv')):
                s.wait_ge(S['hv'], hv_prev)
                s.wait_ge(S['pv'], pv_prev)
                for n in range(32):
                    s.wait_ge(S['bm'], base_bm + n + 1)
                    tgt = (Bm_w[:, n * 512:(n + 1) * 512] if n < 16 else
                           C_w[:, (n - 16) * 512:(n - 15) * 512])
                    s.activation(tgt, ps_bc[n % 2][:, :], AF.Copy
                                 ).then_inc(S['bce'], 1)
            C.bump('bm', 32)
            C.bump('bce', 32)
            PL.add(RawPhase({'tensor': bct, 'scalar': bcs}))
            bce_now = C.g('bce')

            base_ld = [C.g('ld0'), C.g('ld1')]

            def loads(s, base_sc=C.g('sc') - 16 * ci, c0d=c0d, jj0=jj0):
                for j in range(16):
                    if jj0 + j >= 2:
                        s.wait_ge(S['sc'], jj0 + j - 1)
                    sem = S['ld0' if j % 2 == 0 else 'ld1']
                    s.dma_start(out=dl[j % 2][:, :],
                                in_=delta_i[j * 128:(j + 1) * 128,
                                            c0d:c0d + 512]).then_inc(sem, 16)
                    s.dma_start(out=ul[j % 2][:, :],
                                in_=u_i[j * 128:(j + 1) * 128,
                                        c0d:c0d + 512]).then_inc(sem, 16)
                    s.dma_start(out=zl[j % 2][:, :],
                                in_=sz_i[j * 128:(j + 1) * 128,
                                         c0d:c0d + 512]).then_inc(sem, 16)
            C.bump('ld0', 48 * 8)
            C.bump('ld1', 48 * 8)
            PL.add(RawPhase({'sync': loads}))

            def scal(s, jj0=jj0, gg0=gg0, base_ld=base_ld, bce_now=bce_now):
                s.wait_ge(S['cl'], 66)
                for j in range(16):
                    par = j % 2
                    s.wait_ge(S['ld0' if par == 0 else 'ld1'],
                              base_ld[par] + 48 * (j // 2 + 1))
                    for n in range(16):
                        g = gg0 + j * 16 + n
                        s.wait_ge(S['hv'], max(0, g - 1))
                        s.activation(da_b[n % 2][:, :], dl[j % 2][:, :],
                                     AF.Exp, bias=0.0,
                                     scale=nA[:, j * 16 + n:j * 16 + n + 1]
                                     ).then_inc(S['da'], 1)
            C.bump('da', 256)
            PL.add(RawPhase({'scalar': scal}))

            def vec(v, jj0=jj0, gg0=gg0, base_ld=base_ld, bce_now=bce_now):
                for j in range(16):
                    par = j % 2
                    v.wait_ge(S['ld0' if par == 0 else 'ld1'],
                              base_ld[par] + 48 * (j // 2 + 1))
                    if j == 0:
                        v.wait_ge(S['bce'], bce_now)
                    v.tensor_mul(duj[j % 2][:, :], dl[j % 2][:, :],
                                 ul[j % 2][:, :])
                    for n in range(16):
                        g = gg0 + j * 16 + n
                        cidx = j * 16 + n
                        v.wait_ge(S['da'], g + 1)
                        v.wait_ge(S['pv'], max(0, g - 1))
                        v.tensor_mul(db_b[n % 2][:, :], duj[j % 2][:, :],
                                     Bm_w[:, n * 512:(n + 1) * 512])
                        v.tensor_tensor_scan(
                            hb_b[n % 2][:, :], da_b[n % 2][:, :],
                            db_b[n % 2][:, :], hcarry[:, cidx:cidx + 1],
                            ALU.mult, ALU.add)
                        v.tensor_copy(hcarry[:, cidx:cidx + 1],
                                      hb_b[n % 2][:, 511:512]
                                      ).then_inc(S['hv'], 1)
                    v.wait_ge(S['ym'], gg0 + j * 16 + 16)
                    v.scalar_tensor_tensor(tp[j % 2][:, :], ul[j % 2][:, :],
                                           dpc_t[:, j:j + 1], ps_y[j % 2][:, :],
                                           ALU.mult, ALU.add)
                    v.tensor_mul(ybf[j][:, :], tp[j % 2][:, :],
                                 zl[j % 2][:, :]).then_inc(S['sc'], 1)
            C.bump('hv', 256)
            C.bump('sc', 16)
            PL.add(RawPhase({'vector': vec}))

            def pool(p, gg0=gg0):
                for j in range(16):
                    for n in range(16):
                        g = gg0 + j * 16 + n
                        p.wait_ge(S['hv'], g + 1)
                        p.wait_ge(S['ym'], max(0, g - 1))
                        p.tensor_mul(ch_b[n % 2][:, :], hb_b[n % 2][:, :],
                                     C_w[:, n * 512:(n + 1) * 512]
                                     ).then_inc(S['pv'], 1)
            C.bump('pv', 256)
            PL.add(RawPhase({'gpsimd': pool}))

            def tens(t, jj0=jj0, gg0=gg0):
                for j in range(16):
                    if jj0 + j >= 2:
                        t.wait_ge(S['sc'], jj0 + j - 1)
                    for n in range(16):
                        g = gg0 + j * 16 + n
                        t.wait_ge(S['pv'], g + 1)
                        t.matmul(ps_y[j % 2][:, :], i_bf[:, :],
                                 ch_b[n % 2][0:128, :],
                                 start=(n == 0), stop=(n == 15)
                                 ).then_inc(S['ym'], 1)
            C.bump('ym', 256)
            PL.add(RawPhase({'tensor': tens}))

            base_so = [C.g('so0'), C.g('so1')]

            def mk_evo(base_so=base_so, c0d=c0d, ph_holder=[]):
                def evac(engname, eng, ps, n_abs, nr, c0, cw, gi):
                    k = n_abs // 128
                    if engname == 'scalar':
                        par = gi % 2
                        eng.wait_ge(S['so0' if par == 0 else 'so1'],
                                    base_so[par] + 16 * (gi // 2))
                        eng.activation(ob[gi % 2][:, 0:cw], ps[0:nr, 0:cw],
                                       AF.Copy).then_inc(S['ev'], 1)
                    elif engname == 'sync':
                        eng.wait_ge(S['ev'], ph_holder[0].base_ev + gi + 1)
                        eng.dma_start(out=od_o[k * 128:(k + 1) * 128,
                                               c0d:c0d + 512],
                                      in_=ob[gi % 2][:, 0:cw]
                                      ).then_inc(S['so0' if gi % 2 == 0
                                                 else 'so1'], 16)
                return evac, ph_holder

            evo, ho = mk_evo()
            mmo = MMPhase(S, C, woutT_d, DI, D, 512,
                          lambda k: ybf[k][:, :], wpanel, psA, evo,
                          evac_engines=('scalar', 'sync'),
                          extra_tensor_waits=[('sc', C.g('sc'))])
            ho.append(mmo)
            C.bump('so0', 16 * 4)
            C.bump('so1', 16 * 4)
            PL.add(mmo)

        def fin(s, t0=C.g('so0'), t1=C.g('so1')):
            s.wait_ge(S['so0'], t0)
            s.wait_ge(S['so1'], t1)
        PL.add(RawPhase({'sync': fin}))
        block = e(nc.Block())
        PL.emit_all(block)
    return nc


def host_prep_D(inp, c_results):
    eye = np.eye(128, dtype=np.float32)
    oh = np.zeros((32, 32 * 128), np.float32)
    for n in range(32):
        oh[n, n * 128:(n + 1) * 128] = 1.0
    maps = []
    for c in range(NCORES):
        dirb = c % 2
        sfx = 'b' if dirb else 'f'
        alog = np.asarray(inp[f"L1_attn_Alog_{sfx}"], np.float32)
        alogc = f32c(alog.reshape(16, 128, 16).transpose(1, 0, 2)
                     .reshape(128, 256))
        maps.append(dict(
            u=c_results[c]["u"], sz=c_results[c]["sz"],
            delta=c_results[c]["delta"], bc=c_results[c]["bc"],
            alogc=alogc, dpc=cols128(inp[f"L1_attn_Dp_{sfx}"]),
            woutT=f32c(np.asarray(inp[f"L1_attn_wout_{sfx}"], np.float32).T),
            oh32=oh, i128=eye,
        ))
    return maps


def build_E():
    nc = bass.Bass("TRN2", target_bir_lowering=False)
    fwd_i = nc.dram_tensor("fwdh", [D, TLOC], F32, kind="ExternalInput").ap()
    bwd_i = nc.dram_tensor("bwdh", [D, TLOC], F32, kind="ExternalInput").ap()
    ln2_i = nc.dram_tensor("ln2", [D, TLOC], F32, kind="ExternalInput").ap()
    wprojT_d = nc.dram_tensor("wprojT", [DI, D], F32, kind="ExternalInput").ap()
    bprojc_d = nc.dram_tensor("bprojc", [128, 8], F32, kind="ExternalInput").ap()
    ln3_o = nc.dram_tensor("ln3", [D, TLOC], F32, kind="ExternalOutput").ap()

    S = {}
    C = Cnt()
    PL = PhaseList()
    with ExitStack() as cs:
        e = cs.enter_context
        for sname in ('wl0', 'wl1', 'mm', 'ev', 'xl', 'xf', 'cl', 'sv', 'st',
                      'sq', 'squ', 'lnm', 'lnv', 'lns', 'lnb', 'lne'):
            S[sname] = e(nc.semaphore(sname))
        catbf = [e(nc.sbuf_tensor(f"cat{k}", [128, 512], BF16)) for k in range(16)]
        x32 = [e(nc.sbuf_tensor(f"x32{k}", [128, 512], F32)) for k in range(8)]
        t32 = [e(nc.sbuf_tensor(f"t32{i}", [128, 512], F32)) for i in range(2)]
        s32 = [e(nc.sbuf_tensor(f"s32{k}", [128, 512], F32)) for k in range(8)]
        sbf = [e(nc.sbuf_tensor(f"sbf{k}", [128, 512], BF16)) for k in range(8)]
        lnout = [e(nc.sbuf_tensor(f"lno{k}", [128, 512], F32)) for k in range(8)]
        wpanel = [e(nc.sbuf_tensor(f"wp{i}", [128, 16 * 512], BF16)) for i in range(2)]
        bprojc_t = e(nc.sbuf_tensor("bpc", [128, 8], F32))
        aux = dict(
            ones128_bf=e(nc.sbuf_tensor("o128", [128, 1], BF16)),
            ones1_f32=e(nc.sbuf_tensor("o1f", [1, 128], F32)),
            sq_bf=[e(nc.sbuf_tensor(f"sqbf{i}", [128, 512], BF16)) for i in range(2)],
            row_m=e(nc.sbuf_tensor("row_m", [1, 512], F32)),
            row_v=e(nc.sbuf_tensor("row_v", [1, 512], F32)),
            row_t=e(nc.sbuf_tensor("row_t", [1, 512], F32)),
            row_r=e(nc.sbuf_tensor("row_r", [1, 512], F32)),
            eps=e(nc.sbuf_tensor("cst_eps", [1, 1], F32)),
            neghalf=e(nc.sbuf_tensor("cst_nh", [1, 1], F32)),
            nrm=[e(nc.sbuf_tensor(f"nrm{i}", [128, 512], F32)) for i in range(2)],
            ps_m=e(nc.psum_tensor("psm", [1, 512], F32)),
            ps_q=e(nc.psum_tensor("psq", [1, 512], F32)),
            ps_mb=e(nc.psum_tensor("psmb", [128, 512], F32)),
            ps_rb=e(nc.psum_tensor("psrb", [128, 512], F32)),
        )
        psA = [e(nc.psum_tensor(f"psA{i}", [128, 512], F32)) for i in range(2)]

        def consts_g(g):
            g.dma_start(out=bprojc_t[:, :], in_=bprojc_d).then_inc(S['cl'], 16)

        def consts_v(v):
            v.memset(aux['ones1_f32'][0:1, :], 1.0)
            v.memset(aux['ones128_bf'][:, 0:1], 1.0)
            v.memset(aux['eps'][0:1, 0:1], 1e-5)
            v.memset(aux['neghalf'][0:1, 0:1], -0.5)
            v.wait_ge(S['cl'], 16)

        def consts_s(s):
            s.wait_ge(S['cl'], 16)
        C.bump('cl', 16)
        PL.add(RawPhase({'gpsimd': consts_g, 'vector': consts_v,
                         'scalar': consts_s}))

        for ci in range(4):
            c0d = ci * 512
            prev_mm = C.g('mm')
            prev_sv = C.g('sv')

            def loads_g(g, c0d=c0d, ci=ci, prev_mm=prev_mm):
                if ci > 0:
                    g.wait_ge(S['mm'], prev_mm)
                for k in range(8):
                    g.dma_start(out=catbf[k][:, :],
                                in_=fwd_i[k * 128:(k + 1) * 128, c0d:c0d + 512]
                                ).then_inc(S['xl'], 16)
                for k in range(8):
                    g.dma_start(out=catbf[8 + k][:, :],
                                in_=bwd_i[k * 128:(k + 1) * 128, c0d:c0d + 512]
                                ).then_inc(S['xl'], 16)

            def loads_s(s, c0d=c0d, ci=ci, prev_sv=prev_sv):
                if ci > 0:
                    s.wait_ge(S['sv'], prev_sv)
                for k in range(8):
                    s.dma_start(out=x32[k][:, :],
                                in_=ln2_i[k * 128:(k + 1) * 128, c0d:c0d + 512]
                                ).then_inc(S['xf'], 16)
            C.bump('xl', 16 * 16)
            C.bump('xf', 16 * 8)
            xl_now = C.g('xl')
            xf_now = C.g('xf')
            PL.add(RawPhase({'gpsimd': loads_g, 'sync': loads_s}))

            def mk_evp(xf_now=xf_now, ph_holder=[]):
                def evac(engname, eng, ps, n_abs, nr, c0, cw, gi):
                    k = n_abs // 128
                    if engname == 'vector':
                        if gi == 0:
                            eng.wait_ge(S['xf'], xf_now)
                        eng.scalar_tensor_tensor(
                            t32[gi % 2][:, 0:cw], ps[0:nr, 0:cw],
                            bprojc_t[:, k:k + 1], x32[k][:, 0:cw],
                            ALU.add, ALU.add).then_inc(S['ev'], 1)
                        eng.tensor_copy(s32[k][:, 0:cw], t32[gi % 2][:, 0:cw])
                        eng.tensor_copy(sbf[k][:, 0:cw], t32[gi % 2][:, 0:cw]
                                        ).then_inc(S['sv'], 1)
                return evac, ph_holder

            evp, hp = mk_evp()
            mmp = MMPhase(S, C, wprojT_d, DI, D, 512,
                          lambda k: catbf[k][:, :], wpanel, psA, evp,
                          evac_engines=('vector',),
                          extra_tensor_waits=[('xl', xl_now)])
            hp.append(mmp)
            C.bump('sv', 8)
            PL.add(mmp)

            base_st = C.g('st')
            ln = LNPhase(S, C, lambda k: sbf[k], lambda k: s32[k],
                         lambda k: lnout[k], 512, aux,
                         pre={'tensor': [('sv', C.g('sv'))],
                              'scalar': [('sv', C.g('sv'))],
                              'vector': [('st', base_st)]})
            PL.add(ln)

            def store(s, base=ln.base_lne, c0d=c0d):
                for k in range(8):
                    s.wait_ge(S['lne'], base + k + 1)
                    s.dma_start(out=ln3_o[k * 128:(k + 1) * 128, c0d:c0d + 512],
                                in_=lnout[k][:, :]).then_inc(S['st'], 16)
            C.bump('st', 16 * 8)
            PL.add(RawPhase({'sync': store}))

        st_total = C.g('st')
        PL.add(RawPhase({'sync': lambda s: s.wait_ge(S['st'], st_total)}))
        block = e(nc.Block())
        PL.emit_all(block)
    return nc


def host_prep_E(inp, d_results, ln2_by_core):
    wprojT = f32c(np.asarray(inp["L1_attn_wproj"], np.float32).T)
    bprojc = cols128(inp["L1_attn_bproj"])
    maps = []
    for c in range(NCORES):
        b, h = c // 2, c % 2
        fwd_full = d_results[2 * b]["od"]
        bwd_full = d_results[2 * b + 1]["od"][:, ::-1]
        maps.append(dict(
            fwdh=f32c(fwd_full[:, h * TLOC:(h + 1) * TLOC]),
            bwdh=f32c(bwd_full[:, h * TLOC:(h + 1) * TLOC]),
            ln2=ln2_by_core[c], wprojT=wprojT, bprojc=bprojc))
    return maps


def build_F():
    nc = bass.Bass("TRN2", target_bir_lowering=False)
    ln3_i = nc.dram_tensor("ln3", [D, TLOC], F32, kind="ExternalInput").ap()
    w1T_d = nc.dram_tensor("w1T", [D, 4096], F32, kind="ExternalInput").ap()
    w2T_d = nc.dram_tensor("w2T", [D, 4096], F32, kind="ExternalInput").ap()
    h1T_d = nc.dram_tensor("h1T", [4096, 4096], F32, kind="ExternalInput").ap()
    h2T_d = nc.dram_tensor("h2T", [4096, 4096], F32, kind="ExternalInput").ap()
    w3T_d = nc.dram_tensor("w3T", [4096, D], F32, kind="ExternalInput").ap()
    s1c_d = nc.dram_tensor("s1c", [128, 32], F32, kind="ExternalInput").ap()
    s2c_d = nc.dram_tensor("s2c", [128, 32], F32, kind="ExternalInput").ap()
    sh1c_d = nc.dram_tensor("sh1c", [128, 32], F32, kind="ExternalInput").ap()
    sh2c_d = nc.dram_tensor("sh2c", [128, 32], F32, kind="ExternalInput").ap()
    s3c_d = nc.dram_tensor("s3c", [128, 8], F32, kind="ExternalInput").ap()
    y_o = nc.dram_tensor("y", [D, TLOC], F32, kind="ExternalOutput").ap()
    h0_d = nc.dram_tensor("h0d", [4096, TLOC], BF16).ap()
    h1_d = nc.dram_tensor("h1d", [4096, TLOC], BF16).ap()
    h2_d = nc.dram_tensor("h2d", [4096, TLOC], BF16).ap()

    S = {}
    C = Cnt()
    PL = PhaseList()
    with ExitStack() as cs:
        e = cs.enter_context
        for sname in ('wl0', 'wl1', 'mm', 'ev', 'xl', 'xf', 'cl', 'sv', 'st',
                      'sh0', 'sh1', 'sq', 'squ', 'lnm', 'lnv', 'lns', 'lnb',
                      'lne'):
            S[sname] = e(nc.semaphore(sname))
        a32 = [e(nc.sbuf_tensor(f"a32{j}", [128, 512], F32)) for j in range(32)]
        hbf = [a32[j][:, :].bitcast(BF16)[:, 0:512] for j in range(32)]
        xbf = [e(nc.sbuf_tensor(f"xbf{k}", [128, 512], BF16)) for k in range(8)]
        hobuf = [e(nc.sbuf_tensor(f"hob{i}", [128, 512], BF16)) for i in range(2)]
        x32 = [e(nc.sbuf_tensor(f"x32{k}", [128, 512], F32)) for k in range(8)]
        s32 = [e(nc.sbuf_tensor(f"s32{k}", [128, 512], F32)) for k in range(8)]
        sbf = [e(nc.sbuf_tensor(f"sbf{k}", [128, 512], BF16)) for k in range(8)]
        wpanel = [e(nc.sbuf_tensor(f"wp{i}", [128, 32 * 512], BF16)) for i in range(2)]
        s1c = e(nc.sbuf_tensor("s1ct", [128, 32], F32))
        s2c = e(nc.sbuf_tensor("s2ct", [128, 32], F32))
        sh1c = e(nc.sbuf_tensor("sh1ct", [128, 32], F32))
        sh2c = e(nc.sbuf_tensor("sh2ct", [128, 32], F32))
        s3c = e(nc.sbuf_tensor("s3ct", [128, 8], F32))
        aux = dict(
            ones128_bf=e(nc.sbuf_tensor("o128", [128, 1], BF16)),
            ones1_f32=e(nc.sbuf_tensor("o1f", [1, 128], F32)),
            sq_bf=[e(nc.sbuf_tensor(f"sqbf{i}", [128, 512], BF16)) for i in range(2)],
            row_m=e(nc.sbuf_tensor("row_m", [1, 512], F32)),
            row_v=e(nc.sbuf_tensor("row_v", [1, 512], F32)),
            row_t=e(nc.sbuf_tensor("row_t", [1, 512], F32)),
            row_r=e(nc.sbuf_tensor("row_r", [1, 512], F32)),
            eps=e(nc.sbuf_tensor("cst_eps", [1, 1], F32)),
            neghalf=e(nc.sbuf_tensor("cst_nh", [1, 1], F32)),
            nrm=[e(nc.sbuf_tensor(f"nrm{i}", [128, 512], F32)) for i in range(2)],
            ps_m=e(nc.psum_tensor("psm", [1, 512], F32)),
            ps_q=e(nc.psum_tensor("psq", [1, 512], F32)),
            ps_mb=e(nc.psum_tensor("psmb", [128, 512], F32)),
            ps_rb=e(nc.psum_tensor("psrb", [128, 512], F32)),
        )
        psA = [e(nc.psum_tensor(f"psA{i}", [128, 512], F32)) for i in range(2)]

        def consts_g(g):
            for t, d_ in ((s1c, s1c_d), (s2c, s2c_d), (sh1c, sh1c_d),
                          (sh2c, sh2c_d), (s3c, s3c_d)):
                g.dma_start(out=t[:, :], in_=d_).then_inc(S['cl'], 16)

        def consts_v(v):
            v.memset(aux['ones1_f32'][0:1, :], 1.0)
            v.memset(aux['ones128_bf'][:, 0:1], 1.0)
            v.memset(aux['eps'][0:1, 0:1], 1e-5)
            v.memset(aux['neghalf'][0:1, 0:1], -0.5)
            v.wait_ge(S['cl'], 80)

        def consts_s(s):
            s.wait_ge(S['cl'], 80)
        C.bump('cl', 80)
        PL.add(RawPhase({'gpsimd': consts_g, 'vector': consts_v,
                         'scalar': consts_s}))

        for ci in range(4):
            c0d = ci * 512
            prev_mm = C.g('mm')

            def loadx(g, c0d=c0d, ci=ci, prev_mm=prev_mm):
                if ci > 0:
                    g.wait_ge(S['mm'], prev_mm)
                for k in range(8):
                    g.dma_start(out=xbf[k][:, :],
                                in_=ln3_i[k * 128:(k + 1) * 128, c0d:c0d + 512]
                                ).then_inc(S['xl'], 16)
            C.bump('xl', 128)
            xl_now = C.g('xl')
            PL.add(RawPhase({'gpsimd': loadx}))

            def ev1(engname, eng, ps, n_abs, nr, c0, cw, gi):
                j = n_abs // 128
                eng.activation(a32[j][:, 0:cw], ps[0:nr, 0:cw], AF.Silu,
                               bias=0.0, scale=s1c[:, j:j + 1]
                               ).then_inc(S['ev'], 1)

            mm1 = MMPhase(S, C, w1T_d, D, 4096, 512, lambda k: xbf[k][:, :],
                          wpanel, psA, ev1, evac_engines=('scalar',),
                          extra_tensor_waits=[('xl', xl_now)])
            PL.add(mm1)

            base_sh = [C.g('sh0'), C.g('sh1')]

            def mk_ev2(base_sh=base_sh, c0d=c0d, ph_holder=[]):
                def evac(engname, eng, ps, n_abs, nr, c0, cw, gi):
                    j = n_abs // 128
                    if engname == 'vector':
                        par = gi % 2
                        eng.wait_ge(S['sh0' if par == 0 else 'sh1'],
                                    base_sh[par] + 16 * (gi // 2))
                        eng.scalar_tensor_tensor(
                            hobuf[gi % 2][:, 0:cw], ps[0:nr, 0:cw],
                            s2c[:, j:j + 1], a32[j][:, 0:cw],
                            ALU.mult, ALU.mult).then_inc(S['ev'], 1)
                    elif engname == 'sync':
                        eng.wait_ge(S['ev'], ph_holder[0].base_ev + gi + 1)
                        eng.dma_start(out=h0_d[j * 128:(j + 1) * 128,
                                               c0d:c0d + cw],
                                      in_=hobuf[gi % 2][:, 0:cw]
                                      ).then_inc(S['sh0' if gi % 2 == 0
                                                 else 'sh1'], 16)
                return evac, ph_holder

            ev2, h2h = mk_ev2()
            mm2 = MMPhase(S, C, w2T_d, D, 4096, 512, lambda k: xbf[k][:, :],
                          wpanel, psA, ev2, evac_engines=('vector', 'sync'))
            h2h.append(mm2)
            C.bump('sh0', 16 * 16)
            C.bump('sh1', 16 * 16)
            PL.add(mm2)

        for (src_d, dst_d, wT, scol) in ((h0_d, h1_d, h1T_d, sh1c),
                                         (h1_d, h2_d, h2T_d, sh2c)):
            for ci in range(4):
                c0d = ci * 512
                prev_mm = C.g('mm')

                def loadh(s, c0d=c0d, prev_mm=prev_mm, src_d=src_d,
                          prev_ev=C.g('ev')):
                    s.wait_ge(S['mm'], prev_mm)
                    s.wait_ge(S['ev'], prev_ev)
                    for j in range(32):
                        s.dma_start(out=hbf[j],
                                    in_=src_d[j * 128:(j + 1) * 128,
                                              c0d:c0d + 512]
                                    ).then_inc(S['xf'], 16)
                C.bump('xf', 16 * 32)
                xf_now = C.g('xf')
                PL.add(RawPhase({'sync': loadh}))

                base_sh = [C.g('sh0'), C.g('sh1')]

                def mk_evh(base_sh=base_sh, c0d=c0d, dst_d=dst_d, scol=scol,
                           ph_holder=[]):
                    def evac(engname, eng, ps, n_abs, nr, c0, cw, gi):
                        j = n_abs // 128
                        if engname == 'scalar':
                            par = gi % 2
                            eng.wait_ge(S['sh0' if par == 0 else 'sh1'],
                                        base_sh[par] + 16 * (gi // 2))
                            eng.activation(hobuf[gi % 2][:, 0:cw],
                                           ps[0:nr, 0:cw], AF.Gelu,
                                           bias=0.0, scale=scol[:, j:j + 1]
                                           ).then_inc(S['ev'], 1)
                        elif engname == 'sync':
                            eng.wait_ge(S['ev'],
                                        ph_holder[0].base_ev + gi + 1)
                            eng.dma_start(out=dst_d[j * 128:(j + 1) * 128,
                                                    c0d:c0d + cw],
                                          in_=hobuf[gi % 2][:, 0:cw]
                                          ).then_inc(S['sh0' if gi % 2 == 0
                                                     else 'sh1'], 16)
                    return evac, ph_holder

                evh, hh = mk_evh()
                mmh = MMPhase(S, C, wT, 4096, 4096, 512, lambda k: hbf[k],
                              wpanel, psA, evh, evac_engines=('scalar', 'sync'),
                              extra_tensor_waits=[('xf', xf_now)])
                hh.append(mmh)
                C.bump('sh0', 16 * 16)
                C.bump('sh1', 16 * 16)
                PL.add(mmh)

        for ci in range(4):
            c0d = ci * 512
            prev_mm = C.g('mm')
            prev_sv = C.g('sv')

            def loadh2(s, c0d=c0d, prev_mm=prev_mm, prev_sv=prev_sv, ci=ci,
                       prev_ev=C.g('ev')):
                s.wait_ge(S['mm'], prev_mm)
                s.wait_ge(S['ev'], prev_ev)
                if ci > 0:
                    s.wait_ge(S['sv'], prev_sv)
                for j in range(32):
                    s.dma_start(out=hbf[j],
                                in_=h2_d[j * 128:(j + 1) * 128, c0d:c0d + 512]
                                ).then_inc(S['xf'], 16)
                for k in range(8):
                    s.dma_start(out=x32[k][:, :],
                                in_=ln3_i[k * 128:(k + 1) * 128, c0d:c0d + 512]
                                ).then_inc(S['xf'], 16)
            C.bump('xf', 16 * 40)
            xf_now = C.g('xf')
            PL.add(RawPhase({'sync': loadh2}))

            def mk_ev4(xf_now=xf_now, ph_holder=[]):
                def evac(engname, eng, ps, n_abs, nr, c0, cw, gi,
                         st_prev=C.g('st')):
                    k = n_abs // 128
                    if engname == 'vector':
                        if gi == 0:
                            eng.wait_ge(S['xf'], xf_now)
                            eng.wait_ge(S['st'], st_prev)
                        eng.scalar_tensor_tensor(
                            s32[k][:, 0:cw], ps[0:nr, 0:cw], s3c[:, k:k + 1],
                            x32[k][:, 0:cw], ALU.mult, ALU.add
                            ).then_inc(S['ev'], 1)
                        eng.tensor_copy(sbf[k][:, 0:cw], s32[k][:, 0:cw]
                                        ).then_inc(S['sv'], 1)
                return evac, ph_holder

            ev4, h4 = mk_ev4()
            mm4 = MMPhase(S, C, w3T_d, 4096, D, 512, lambda k: hbf[k],
                          wpanel, psA, ev4, evac_engines=('vector',),
                          extra_tensor_waits=[('xf', xf_now)])
            h4.append(mm4)
            C.bump('sv', 8)
            PL.add(mm4)

            base_st = C.g('st')
            ln = LNPhase(S, C, lambda k: sbf[k], lambda k: s32[k],
                         lambda k: s32[k], 512, aux,
                         pre={'tensor': [('sv', C.g('sv'))],
                              'scalar': [('sv', C.g('sv'))],
                              'vector': [('st', base_st)]})
            PL.add(ln)

            def store(s, base=ln.base_lne, c0d=c0d):
                for k in range(8):
                    s.wait_ge(S['lne'], base + k + 1)
                    s.dma_start(out=y_o[k * 128:(k + 1) * 128, c0d:c0d + 512],
                                in_=s32[k][:, :]).then_inc(S['st'], 16)
            C.bump('st', 16 * 8)
            PL.add(RawPhase({'sync': store}))

        st_total = C.g('st')
        PL.add(RawPhase({'sync': lambda s: s.wait_ge(S['st'], st_total)}))
        block = e(nc.Block())
        PL.emit_all(block)
    return nc


def host_prep_F(inp, e_results):
    def colsN(v):
        return f32c(np.asarray(v, np.float32).reshape(-1, 128).T)
    m = dict(
        w1T=f32c(np.asarray(inp["L1_ffn_w1_w"], np.float32).T),
        w2T=f32c(np.asarray(inp["L1_ffn_w2_w"], np.float32).T),
        h1T=f32c(np.asarray(inp["L1_ffn_h1_w"], np.float32).T),
        h2T=f32c(np.asarray(inp["L1_ffn_h2_w"], np.float32).T),
        w3T=f32c(np.asarray(inp["L1_ffn_w3_w"], np.float32).T),
        s1c=colsN(inp["L1_ffn_w1_s"]), s2c=colsN(inp["L1_ffn_w2_s"]),
        sh1c=colsN(inp["L1_ffn_h1_s"]), sh2c=colsN(inp["L1_ffn_h2_s"]),
        s3c=colsN(inp["L1_ffn_w3_s"]),
    )
    return [dict(m, ln3=e_results[c]["ln3"]) for c in range(NCORES)]


def kernel(**inputs):
    cores = list(range(NCORES))
    r1 = run_bass_kernel_spmd(build_A1(), host_prep_A1(inputs), core_ids=cores).results
    r2 = run_bass_kernel_spmd(build_A2(), host_prep_A2(inputs, r1), core_ids=cores).results
    r3 = run_bass_kernel_spmd(build_B(), host_prep_B(inputs, r2), core_ids=cores).results
    ln2_by_core = [np.asarray(r3[c]["ln2"], np.float32) for c in range(NCORES)]
    rC = run_bass_kernel_spmd(build_C(), host_prep_C(inputs, ln2_by_core), core_ids=cores).results
    rD = run_bass_kernel_spmd(build_D(), host_prep_D(inputs, rC), core_ids=cores).results
    rE = run_bass_kernel_spmd(build_E(), host_prep_E(inputs, rD, ln2_by_core), core_ids=cores).results
    rF = run_bass_kernel_spmd(build_F(), host_prep_F(inputs, rE), core_ids=cores).results
    out = np.empty((B, T, D), np.float32)
    for c in range(NCORES):
        b, h = c // 2, c % 2
        out[b, h * TLOC:(h + 1) * TLOC, :] = rF[c]["y"].T
    return out
